# version 10
# baseline (speedup 1.0000x reference)
import numpy as np
import ml_dtypes
import concourse.bass as bass
import concourse.mybir as mybir
from contextlib import ExitStack
from concourse.bass_utils import run_bass_kernel_spmd

F32 = mybir.dt.float32
BF16 = mybir.dt.bfloat16
U8 = mybir.dt.uint8
ALU = mybir.AluOpType
AF = mybir.ActivationFunctionType
AX = mybir.AxisListType

B, T, E, D = 8, 4096, 1024, 64
NCH = E // 128
NT = T // 128
NKC = T // 128
EPS = 1e-5

ENGS = ("pe", "act", "dve", "pool", "sp")
DSEMS = tuple(f"dx{t}" for t in range(32)) + ("dwm0", "dwm1", "dfb", "deye", "deyb", "dc", "dw1", "dw2", "dw3", "dqk", "do", "ds")


class Prog:
    def __init__(self):
        self.stream = {e: [] for e in ENGS}
        self.cnt = {s: 0 for s in ENGS + DSEMS}
        self.mile = {}
        self.lastwait = {}

    def _waits(self, eng, wait):
        ws = []
        for w in wait:
            if w is None:
                continue
            sem, val, peng = self.mile[w] if isinstance(w, str) else w
            if peng == eng and sem in ENGS:
                continue
            key = (eng, sem)
            if self.lastwait.get(key, -1) >= val:
                continue
            self.lastwait[key] = val
            ws.append((sem, val))
        return ws

    def emit(self, eng, fn, wait=(), inc=None, tag=None):
        ws = self._waits(eng, wait)
        amt = 0
        if inc is not None:
            amt = 16 if inc in DSEMS else 1
            self.cnt[inc] += amt
            if tag is not None:
                self.mile[tag] = (inc, self.cnt[inc], eng)
        elif tag is not None:
            raise ValueError("tag requires inc")

        def closure(e, sems):
            for sem, val in ws:
                e.wait_ge(sems[sem], val)
            inst = fn(e)
            if inc is not None:
                inst.then_inc(sems[inc], amt)
        self.stream[eng].append(closure)

    def wait(self, eng, wait):
        ws = self._waits(eng, wait)
        if not ws:
            return

        def closure(e, sems):
            for sem, val in ws:
                e.wait_ge(sems[sem], val)
        self.stream[eng].append(closure)

    def lower(self, nc, sems):
        with nc.Block() as block:
            for eng, deco in (("pe", block.tensor), ("act", block.scalar),
                              ("dve", block.vector), ("pool", block.gpsimd),
                              ("sp", block.sync)):
                stream = self.stream[eng]

                def run(e, stream=stream):
                    for closure in stream:
                        closure(e, sems)
                deco(run)


def build(debug=False):
    nc = bass.Bass("TRN2", target_bir_lowering=False)

    x_hbm = nc.dram_tensor("x", [T, E], F32, kind="ExternalInput")
    wq_h = nc.dram_tensor("wq", [E, D], F32, kind="ExternalInput")
    wk_h = nc.dram_tensor("wk", [E, D], F32, kind="ExternalInput")
    wv_h = nc.dram_tensor("wv", [E, D], F32, kind="ExternalInput")
    w1_h = nc.dram_tensor("w1", [E, E], F32, kind="ExternalInput")
    w2_h = nc.dram_tensor("w2", [E, E], F32, kind="ExternalInput")
    w3_h = nc.dram_tensor("w3", [E, E], F32, kind="ExternalInput")
    fb_h = nc.dram_tensor("fbias", [1, E], F32, kind="ExternalInput")
    c_h = nc.dram_tensor("crow", [1, E], F32, kind="ExternalInput")
    wm_h = nc.dram_tensor("wmod", [E, 6 * E], F32, kind="ExternalInput")
    eye_h = nc.dram_tensor("eye", [128, 128], F32, kind="ExternalInput")
    eyb_h = nc.dram_tensor("eyeb", [128, 128], BF16, kind="ExternalInput")
    out_h = nc.dram_tensor("out", [T, E], F32, kind="ExternalOutput")
    dbg = {}
    if debug:
        dbg["mvec"] = nc.dram_tensor("d_mvec", [128, 96], F32, kind="ExternalOutput")
        dbg["qv"] = nc.dram_tensor("d_qv", [128, T], F32, kind="ExternalOutput")
        dbg["kk"] = nc.dram_tensor("d_kk", [128, T], F32, kind="ExternalOutput")
        dbg["stats"] = nc.dram_tensor("d_stats", [128, 128], F32, kind="ExternalOutput")
        dbg["z"] = nc.dram_tensor("d_z", [128, 64], F32, kind="ExternalOutput")
        dbg["av"] = nc.dram_tensor("d_av", [128, T], F32, kind="ExternalOutput")
        dbg["x1"] = nc.dram_tensor("d_x1", [128, NCH * T], F32, kind="ExternalOutput")
        dbg["wc"] = nc.dram_tensor("d_wc", [128, NCH * E], F32, kind="ExternalOutput")
        dbg["u3"] = nc.dram_tensor("d_u3", [128, 32], F32, kind="ExternalOutput")
        dbg["xf"] = nc.dram_tensor("d_xf", [128, NCH * T], F32, kind="ExternalOutput")
        dbg["hs"] = nc.dram_tensor("d_hs", [128, 128], F32, kind="ExternalOutput")
        dbg["ln1"] = nc.dram_tensor("d_ln1", [128, 64], F32, kind="ExternalOutput")

    es = ExitStack()
    sb = lambda n, sh, dt=F32: es.enter_context(nc.sbuf_tensor(n, sh, dt))

    xfm = sb("xfm", [128, NCH * T])
    fbrow = sb("fbrow", [1, E])
    hstat = sb("hstat", [128, 128])
    rntok = sb("rntok", [128, 64])
    rn2 = sb("rn2", [128, 64])
    modf = sb("modf", [128, 48])
    mvec = sb("mvec", [128, 48])
    colsb = sb("colsb", [128, 16])
    uqv = sb("uqv", [128, 2])
    ukk = sb("ukk", [64, 2])
    u3b = sb("u3b", [128, 16])
    bias3 = sb("bias3", [128, 8])
    fbf = sb("fbf", [128, 8])
    negmu = sb("negmu", [128, 2])
    zpart = sb("zpart", [128, 8])
    zval = sb("zval", [128, 2])
    zrec = sb("zrec", [128, 2])
    dmy = sb("dmy", [128, 2])
    eyef = sb("eyef", [128, 128])

    w1s = sb("w1s", [128, NCH * E], BF16)
    w2s = sb("w2s", [128, NCH * E], BF16)
    w3s = sb("w3s", [128, NCH * E], BF16)
    m1s = sb("m1s", [128, NCH * E], BF16)
    scrb = sb("scrb", [128, 2048], BF16)
    vscb = sb("vscb", [128, 2 * D], BF16)
    stA = sb("stA", [128, NCH * 128], BF16)
    stB = sb("stB", [128, NCH * 64], BF16)
    extq = sb("extq", [128, 16], BF16)
    extf = sb("extf", [128, 16], BF16)
    onec = sb("onec", [128, 2], BF16)
    oner = sb("oner", [1, 128], BF16)
    csfm = sb("csfm", [128, 8], BF16)
    eyeb = sb("eyebs", [128, 128], BF16)
    gmask = sb("gmask", [128, 16], U8)

    ps = es.enter_context(nc.psum_tensor("ps", [128, 4096], F32))

    semh = {}
    for s in ENGS + DSEMS:
        semh[s] = es.enter_context(nc.semaphore("sem_" + s))

    p = Prog()

    xfm3 = xfm[:, :].rearrange("p (c m) -> p c m", m=T)
    w1v = w1s[:, :].rearrange("p (c m) -> p c m", m=E)
    w2v = w2s[:, :].rearrange("p (c m) -> p c m", m=E)
    w3v = w3s[:, :].rearrange("p (c m) -> p c m", m=E)
    m1v = m1s[:, :].rearrange("p (c m) -> p c m", m=E)
    stA3 = stA[:, :].rearrange("p (c m) -> p c m", m=128)
    stB3 = stB[:, :].rearrange("p (c m) -> p c m", m=64)

    qv = w1s[:, 0:T]
    kk = w1s[:, T:2 * T]
    qb3 = w2s[:, 0:4096].rearrange("p (c m) -> p c m", m=512)
    avs = w2s[:, 0:2048]
    avd = w2s[:, 4096:8192]
    plbuf = w2s[:, 0:2048]
    sqb = w2s[:, 2048:4096]
    tnh = w2s[:, 4096:6144]
    a3 = m1s[:, :].rearrange("p (c m) -> p c m", m=T)
    xs3 = m1s[:, :].rearrange("p (c m) -> p c m", m=1024)
    csrow = scrb[0:1, 0:E]
    reps = scrb[:, 0:2048]
    vtok = scrb[:, 0:2048]
    wmbuf = m1s[:, 4096:8192]
    modrow = fbrow[0:1, 0:1024]
    rowbf = m1s[0:1, 0:2048]
    ln2rows = scrb[0:1, 0:2048]
    jitb = stA[:, 0:1024]
    bnb = colsb[:, 0:12]

    def bank(i, lo=0, hi=128):
        return ps[lo:hi, 512 * i:512 * (i + 1)]

    p.emit("sp", lambda e: e.dma_start(fbrow[:, :], fb_h[:, :]), inc="dfb", tag="ld_fb")
    p.emit("sp", lambda e: e.dma_start(eyef[:, :], eye_h[:, :]), inc="deye", tag="ld_eye")
    p.emit("sp", lambda e: e.dma_start(eyeb[:, :], eyb_h[:, :]), inc="deyb", tag="ld_eyb")

    p.emit("pool", lambda e: e.dma_start(csrow, c_h[:, :]), inc="dc", tag="ld_c")
    p.emit("pool", lambda e: e.dma_start(
        w2v[:, :, :], w2_h.ap().rearrange("(c p) d -> p c d", p=128)),
        inc="dw2", tag="ld_w2")
    p.emit("pool", lambda e: e.dma_start(
        w3v[:, :, :], w3_h.ap().rearrange("(c p) d -> p c d", p=128)),
        inc="dw3", tag="ld_w3")
    p.emit("pool", lambda e: e.dma_start(
        w1v[:, :, :], w1_h.ap().rearrange("(c p) d -> p c d", p=128)),
        inc="dw1", tag="ld_w1")

    for t in range(NT):
        p.emit("sp", lambda e, t=t: e.dma_start(
            xfm3[:, :, 128 * t:128 * (t + 1)],
            x_hbm[128 * t:128 * (t + 1), :].rearrange("p (c m) -> p c m", m=128)),
            inc=f"dx{t}", tag=f"xld{t}")

    p.emit("dve", lambda e: e.memset(onec[:, :], 1.0))
    p.emit("dve", lambda e: e.memset(dmy[:, :], 1.0))
    p.emit("dve", lambda e: e.memset(oner[:, :], 1.0), inc="dve", tag="consts")

    jslots = (jitb, scrb[:, 1024:2048])
    def mul_jit(src3, rhs3, dst3, first_wait, tg):
        nt = 0
        for oc in range(8):
            jb = jslots[oc % 2]
            for k in range(8):
                w = []
                if k == 0:
                    w = list(first_wait) if oc == 0 else []
                    if oc >= 1:
                        w.append(f"{tg}_je{oc-1}")
                col = 1024 + 512 * (k // 4) + 128 * (k % 4)
                p.emit("pe", lambda e, s=src3, oc=oc, k=k, col=col: e.matmul(
                    ps[:, col:col + 128], s[:, oc, 128 * k:128 * (k + 1)],
                    eyeb[:, :], start=True, stop=True),
                    wait=w, inc=("pe" if k == 7 else None),
                    tag=(f"{tg}_jt{oc}" if k == 7 else None))
            for g in range(2):
                w = [f"{tg}_jt{oc}"]
                if g == 0 and nt >= 4:
                    w.append(f"{tg}_m{nt-3}")
                p.emit("act", lambda e, g=g, jb=jb: e.copy(
                    jb[:, 512 * g:512 * (g + 1)],
                    ps[:, 512 * (2 + g):512 * (2 + g) + 512]),
                    wait=w, inc=("act" if g == 1 else None),
                    tag=(f"{tg}_je{oc}" if g == 1 else None))
            for n in range(2):
                bk = 4 + nt % 2
                for k in range(8):
                    w = []
                    if k == 0:
                        w.append(f"{tg}_je{oc}")
                        if nt >= 2:
                            w.append(f"{tg}_e{nt-2}")
                        if oc == 0 and n == 0:
                            w += list(first_wait)
                    p.emit("pe", lambda e, r=rhs3, oc=oc, n=n, k=k, bk=bk, jb=jb:
                           e.matmul(bank(bk), jb[:, 128 * k:128 * (k + 1)],
                                    r[:, k, 512 * n:512 * (n + 1)],
                                    start=(k == 0), stop=(k == 7)),
                           wait=w, inc=("pe" if k == 7 else None),
                           tag=(f"{tg}_m{nt}" if k == 7 else None))
                p.emit("act", lambda e, d=dst3, oc=oc, n=n, bk=bk: e.copy(
                    d[:, oc, 512 * n:512 * (n + 1)], bank(bk)),
                    wait=[f"{tg}_m{nt}"], inc="act", tag=f"{tg}_e{nt}")
                nt += 1
            yield
        p.wait("pe", [f"{tg}_e{nt-1}"])

    cb_gen = mul_jit(w2v, w3v, m1v, ["ld_w2", "ld_w3", "ld_eyb"], "cB")
    cd_gen = mul_jit(w1v, m1v, w3v, ["ld_w1", "cB_e15"], "cD")
    CBSET = {8, 10, 12, 14, 16, 18, 20, 22}
    CDSET = {24, 26, 28, 30}

    for t in range(NT):
        for j in range(NCH):
            p.emit("dve", lambda e, t=t, j=j: e.bn_stats(
                rn2[:, 6 * j:6 * j + 6], xfm3[:, j, 128 * t:128 * (t + 1)]),
                wait=([f"xld{t}"] if j == 0 else []))
        p.emit("dve", lambda e: e.drain())
        me = rn2[:, 1:48:6]
        mo = rn2[:, 4:48:6]
        Me = rn2[:, 2:48:6]
        Mo = rn2[:, 5:48:6]
        h96 = hstat[:, 96:104]
        h104 = hstat[:, 104:112]
        h112 = hstat[:, 112:120]
        h120 = hstat[:, 120:128]
        p.emit("dve", lambda e, me=me: e.tensor_tensor(h104, me, me, op=ALU.mult))
        p.emit("dve", lambda e, mo=mo: e.tensor_tensor(h112, mo, mo, op=ALU.mult))
        p.emit("dve", lambda e, me=me, mo=mo: e.tensor_tensor(h96, me, mo,
                                                              op=ALU.add))
        p.emit("dve", lambda e: e.tensor_tensor(h104, h104, h112, op=ALU.add))
        p.emit("dve", lambda e, Me=Me, Mo=Mo: e.tensor_tensor(h120, Me, Mo,
                                                              op=ALU.add))
        bases = ((96, 3 * t), (104, 3 * t + 1), (120, 3 * t + 2))
        for b_, _ in bases:
            hb = hstat[:, b_:b_ + 8]
            p.emit("dve", lambda e, hb=hb: e.tensor_tensor(
                hb[:, 0:4], hb[:, 0:4], hb[:, 4:8], op=ALU.add))
        for b_, _ in bases:
            hb = hstat[:, b_:b_ + 8]
            p.emit("dve", lambda e, hb=hb: e.tensor_tensor(
                hb[:, 0:2], hb[:, 0:2], hb[:, 2:4], op=ALU.add))
        for b_, dstc in bases:
            hb = hstat[:, b_:b_ + 8]
            p.emit("dve", lambda e, hb=hb, dstc=dstc: e.tensor_tensor(
                hstat[:, dstc:dstc + 1], hb[:, 0:1], hb[:, 1:2], op=ALU.add),
                inc=("dve" if b_ == 120 else None),
                tag=(f"xbn{t}" if b_ == 120 else None))
        for j in range(NCH):
            w = [f"xld{t}", "ld_eye", f"xbn{t}"]
            if j == 0 and t >= 1:
                w.append(f"xE{t-1}")
            col = 512 * (j // 4) + 128 * (j % 4)
            p.emit("pe", lambda e, t=t, j=j, c=col: e.transpose(
                ps[:, c:c + 128], xfm3[:, j, 128 * t:128 * (t + 1)], eyef[:, :]),
                wait=w, inc=("pe" if j == NCH - 1 else None),
                tag=(f"xT{t}" if j == NCH - 1 else None))
        for g in range(2):
            dst = xfm3[:, 4 * g:4 * g + 4, 128 * t:128 * t + 128]
            p.emit("act", lambda e, d=dst, c=512 * g: e.copy(
                d, ps[:, c:c + 512]),
                wait=[f"xT{t}"], inc=("act" if g == 1 else None),
                tag=(f"xE{t}" if g == 1 else None))
        if t in CBSET:
            next(cb_gen)
        if t in CDSET:
            next(cd_gen)

    msu = hstat[:, 0:96:3]
    m2u = hstat[:, 1:96:3]
    M2u = hstat[:, 2:96:3]
    s_, d_ = rn2[:, 0:32], rn2[:, 32:64]
    ex2 = hstat[:, 96:128]
    p.emit("dve", lambda e: e.drain(), wait=[f"xbn{NT-1}"])
    p.emit("dve", lambda e, m2u=m2u: e.tensor_scalar(
        ex2, m2u, 1.0 / 16, None, op0=ALU.mult))
    p.emit("dve", lambda e, msu=msu: e.tensor_scalar(
        s_, msu, 1.0 / 16, None, op0=ALU.mult))
    p.emit("dve", lambda e, M2u=M2u: e.scalar_tensor_tensor(
        ex2, M2u, 1.0 / 1024, ex2, op0=ALU.mult, op1=ALU.add))
    p.emit("dve", lambda e: e.tensor_tensor(d_, s_, s_, op=ALU.mult))
    p.emit("dve", lambda e: e.tensor_scalar(
        dmy[:, 0:1], dmy[:, 1:2], 1.0, None, op0=ALU.mult))
    p.emit("dve", lambda e: e.tensor_tensor(d_, ex2, d_, op=ALU.subtract))
    p.emit("dve", lambda e: e.tensor_scalar(
        dmy[:, 0:1], dmy[:, 1:2], 1.0, None, op0=ALU.mult))
    p.emit("dve", lambda e: e.tensor_scalar(d_, d_, EPS, None, op0=ALU.add))
    p.emit("dve", lambda e: e.drain())
    p.emit("dve", lambda e: e.reciprocal(d_, d_))
    p.emit("dve", lambda e: e.drain())
    p.emit("dve", lambda e: e.tensor_scalar(
        dmy[:, 0:1], dmy[:, 1:2], 1.0, None, op0=ALU.mult), inc="dve",
        tag="ln1v")
    p.emit("act", lambda e: e.activation(rntok[:, 0:64:2], d_, AF.Sqrt),
           wait=["ln1v"], inc="act", tag="ln1r")
    p.emit("dve", lambda e: e.tensor_scalar(
        rntok[:, 1:64:2], s_, -1.0, None, op0=ALU.mult),
        wait=["ln1r"], inc="dve", tag="ln1done")
    if debug:
        p.emit("pool", lambda e: e.dma_start(dbg["stats"][:, 0:64], rntok[:, :]),
               wait=["ln1done"], inc="ds")
        p.emit("pool", lambda e: e.dma_start(dbg["hs"][:, :], hstat[:, :]),
               wait=[f"xbn{NT-1}"], inc="ds")
        p.emit("pool", lambda e: e.dma_start(dbg["ln1"][:, :], rn2[:, 0:64]),
               wait=["ln1v"], inc="ds")
        p.emit("pool", lambda e: e.dma_start(dbg["xf"][:, :], xfm[:, :]),
               wait=[f"xE{NT-1}", f"xbn{NT-1}"], inc="ds")

    for _ in cb_gen:
        pass
    for _ in cd_gen:
        pass

    for j in range(8):
        p.emit("pe", lambda e, j=j: e.transpose(
            ps[:, 3080 + j:3080 + j + 1], fbrow[0:1, 128 * j:128 * (j + 1)],
            eyef[0:1, 0:1]),
            wait=(["ld_fb", "ld_eye", "cD_e15"] if j == 0 else []),
            inc=("pe" if j == 7 else None), tag=("fbT" if j == 7 else None))
    p.emit("dve", lambda e: e.tensor_copy(fbf[:, :], ps[:, 3080:3088]),
           wait=["fbT"], inc="dve", tag="fbE")
    p.emit("act", lambda e: e.activation(csrow, csrow, AF.Silu),
           wait=["ld_c"], inc="act", tag="silu")
    for j in range(8):
        p.emit("pe", lambda e, j=j: e.matmul(
            ps[:, 3072 + j:3072 + j + 1], csrow[0:1, 128 * j:128 * (j + 1)],
            eyeb[0:1, 0:1], start=True, stop=True),
            wait=(["silu", "ld_eyb", "fbE"] if j == 0 else []),
            inc=("pe" if j == 7 else None), tag=("csT" if j == 7 else None))
    p.emit("dve", lambda e: e.tensor_copy(csfm[:, :], ps[:, 3072:3080]),
           wait=["csT"], inc="dve", tag="csE")

    for pm in range(3):
        for ec in range(8):
            k = 8 * pm + ec
            w = [f"modm{(k-2)//8}_{(k-2)%8}"] if k >= 2 else ["cD_m15"]
            p.emit("pool", lambda e, pm=pm, ec=ec, s=ec % 2: e.dma_start(
                wmbuf[:, 2048 * s:2048 * (s + 1)],
                wm_h[128 * ec:128 * (ec + 1), 2048 * pm:2048 * (pm + 1)]),
                wait=w, inc=f"dwm{k % 2}", tag=f"wmld{k}")
            for n in range(4):
                w = [f"wmld{k}", "csE"]
                if pm > 0 and ec == 0 and n == 0:
                    w.append(f"modrowE{2*pm-1}")
                if pm == 0 and ec == 0 and n == 0:
                    w.append("cD_e15")
                p.emit("pe", lambda e, s=ec % 2, n=n, ec=ec: e.matmul(
                    ps[0:1, 512 * n:512 * (n + 1)], csfm[:, ec:ec + 1],
                    wmbuf[:, 2048 * s + 512 * n:2048 * s + 512 * (n + 1)],
                    start=(ec == 0), stop=(ec == 7)),
                    wait=w, inc=("pe" if n == 3 else None),
                    tag=(f"modm{pm}_{ec}" if n == 3 else None))
        if pm == 0:
            p.emit("pool", lambda e: e.dma_start(
                stA3[:, :, 0:64], wq_h.ap().rearrange("(c p) d -> p c d", p=128)),
                wait=["cD_m15"], inc="dqk", tag="ld_wq")
            p.emit("pool", lambda e: e.dma_start(
                stA3[:, :, 64:128], wv_h.ap().rearrange("(c p) d -> p c d", p=128)),
                inc="dqk", tag="ld_wv")
            p.emit("pool", lambda e: e.dma_start(
                stB3[:, :, :], wk_h.ap().rearrange("(c p) d -> p c d", p=128)),
                inc="dqk", tag="ld_wk")
        for hh2 in range(2):
            sp2 = 2 * pm + hh2
            p.emit("dve", lambda e, hh2=hh2: e.tensor_copy(
                modrow, ps[0:1, 1024 * hh2:1024 * (hh2 + 1)]),
                wait=[f"modm{pm}_7"], inc="dve", tag=f"modrowE{sp2}")
            for blk in range(8):
                w = []
                if blk == 0:
                    w = [f"modrowE{sp2}"]
                    if sp2 > 0:
                        w.append(f"mexE{sp2-1}")
                p.emit("pe", lambda e, blk=blk: e.transpose(
                    ps[:, 3072 + blk:3072 + blk + 1],
                    modrow[0:1, 128 * blk:128 * (blk + 1)], eyef[0:1, 0:1]),
                    wait=w, inc=("pe" if blk == 7 else None),
                    tag=(f"mexT{sp2}" if blk == 7 else None))
            p.emit("dve", lambda e, sp2=sp2: e.tensor_copy(
                modf[:, 8 * sp2:8 * (sp2 + 1)], ps[:, 3072:3080]),
                wait=[f"mexT{sp2}"], inc="dve", tag=f"mexE{sp2}")

    def derive(oi, oo, tagid, wtag):
        sa_i, ca_i, ga_i = oi, oi + 8, oi + 16
        pa, sp_, gp = oo, oo + 8, oo + 16
        p.emit("dve", lambda e: e.tensor_scalar(
            mvec[:, pa:pa + 8], modf[:, ca_i:ca_i + 8], 1.0, None, op0=ALU.add),
            wait=[wtag])
        p.emit("dve", lambda e: e.tensor_single_scalar(
            gmask[:, 0:8], modf[:, ga_i:ga_i + 8], 0.0, op=ALU.is_ge))
        p.emit("dve", lambda e: e.tensor_single_scalar(
            colsb[:, 0:8], modf[:, ga_i:ga_i + 8], 0.25, op=ALU.max))
        p.emit("dve", lambda e: e.tensor_single_scalar(
            colsb[:, 8:16], modf[:, ga_i:ga_i + 8], -0.25, op=ALU.min))
        p.emit("dve", lambda e: e.reciprocal(colsb[:, 8:16] if False else
                                             gmask[:, 8:16] if False else
                                             rn2[:, 0:8], mvec[:, pa:pa + 8]))
        p.emit("dve", lambda e: e.select(
            mvec[:, gp:gp + 8], gmask[:, 0:8], colsb[:, 0:8], colsb[:, 8:16]))
        p.emit("dve", lambda e: e.drain())
        p.emit("dve", lambda e: e.tensor_tensor(
            mvec[:, sp_:sp_ + 8], modf[:, sa_i:sa_i + 8], rn2[:, 0:8],
            op=ALU.mult),
            inc="dve", tag=tagid)

    derive(0, 0, "mva", "mexE2")
    derive(24, 24, "mvb", "mexE5")

    for ec in range(8):
        p.emit("dve", lambda e, ec=ec: e.tensor_scalar(
            stA3[:, ec, :], stA3[:, ec, :], mvec[:, ec:ec + 1], None, op0=ALU.mult),
            wait=(["mva", "ld_wq", "ld_wv", "ld_wk"] if ec == 0 else []))
        p.emit("dve", lambda e, ec=ec: e.tensor_scalar(
            stB3[:, ec, :], stB3[:, ec, :], mvec[:, ec:ec + 1], None, op0=ALU.mult),
            inc=("dve" if ec == 7 else None), tag=("stready" if ec == 7 else None))
    for ec in range(8):
        p.emit("dve", lambda e, ec=ec: e.tensor_scalar(
            w3v[:, ec, :], w3v[:, ec, :], mvec[:, 24 + ec:24 + ec + 1], None,
            op0=ALU.mult),
            wait=(["mvb", "cD_e15"] if ec == 0 else []),
            inc=("dve" if ec == 7 else None), tag=("wcready" if ec == 7 else None))
    if debug:
        p.emit("pool", lambda e: e.dma_start(dbg["mvec"][:, 0:48], mvec[:, :]),
               wait=["mva", "mvb"], inc="ds")
        p.emit("pool", lambda e: e.dma_start(dbg["mvec"][:, 48:96], modf[:, :]),
               inc="ds")
        p.emit("pool", lambda e: e.dma_start(dbg["wc"][:, :], w3s[:, :]),
               wait=["wcready"], inc="ds")
    for ec in range(8):
        p.emit("dve", lambda e, ec=ec: e.memset(extq[:, 2 * ec:2 * ec + 1], 1.0),
               wait=(["mva", "mvb"] if ec == 0 else []))
        p.emit("dve", lambda e, ec=ec: e.memset(extf[:, 2 * ec:2 * ec + 1], 1.0))
        p.emit("dve", lambda e, ec=ec: e.tensor_copy(
            extq[:, 2 * ec + 1:2 * ec + 2], mvec[:, 8 + ec:8 + ec + 1]))
        p.emit("dve", lambda e, ec=ec: e.tensor_copy(
            extf[:, 2 * ec + 1:2 * ec + 2], mvec[:, 32 + ec:32 + ec + 1]),
            inc=("dve" if ec == 7 else None), tag=("extready" if ec == 7 else None))

    prevtag = None
    prevfix = None
    prevwA = None
    prevwB = None
    for q in range(4):
        for r in range(2):
            for u in range(8):
                tgl = 2 * (8 * q + u) + r
                w = []
                if u == 0:
                    w = ["ln1done"]
                    if q == 0 and r == 0:
                        w += ["mexE5", "fbE", "csE"]
                    if r == 1:
                        w.append(f"q_repE{q}_0")
                    elif q > 0:
                        w.append(f"q_repE{q-1}_1")
                p.emit("pe", lambda e, u=u, tgl=tgl: e.transpose(
                    ps[0:1, 2560 + 128 * u:2560 + 128 * (u + 1)],
                    rntok[:, tgl:tgl + 1], eyef[:, :]),
                    wait=w, inc=("pe" if u == 7 else None),
                    tag=(f"q_rT{q}_{r}" if u == 7 else None))
            p.emit("dve", lambda e: e.tensor_copy(rowbf[0:1, 0:1024],
                                                  ps[0:1, 2560:3584]),
                   wait=[f"q_rT{q}_{r}"], inc="dve", tag=f"q_rE{q}_{r}")
            for n in range(2):
                p.emit("pe", lambda e, n=n: e.matmul(
                    ps[:, 2560 + 512 * n:2560 + 512 * (n + 1)], oner[:, :],
                    rowbf[0:1, 512 * n:512 * (n + 1)], start=True, stop=True),
                    wait=([f"q_rE{q}_{r}", "consts"] if n == 0 else []),
                    inc=("pe" if n == 1 else None),
                    tag=(f"q_bT{q}_{r}" if n == 1 else None))
            p.emit("dve", lambda e, r=r: e.tensor_copy(
                reps[:, 1024 * r:1024 * (r + 1)], ps[:, 2560:3584]),
                wait=[f"q_bT{q}_{r}"], inc="dve", tag=f"q_repE{q}_{r}")

        for hh in range(2):
            cg = 1024 * q + 512 * hh
            for ec in range(8):
                w = []
                if ec == 0:
                    w = [f"xE{NT-1}", "stready"]
                    if prevtag is not None:
                        w.append(prevtag)
                p.emit("dve", lambda e, ec=ec, cg=cg: e.tensor_copy(
                    qb3[:, ec, :], xfm3[:, ec, cg:cg + 512]),
                    wait=w, inc=("dve" if ec == 7 else None),
                    tag=(f"q_c{q}_{hh}" if ec == 7 else None))
            for ec in range(8):
                w = [f"q_c{q}_{hh}"]
                if ec == 0 and prevfix is not None:
                    w.append(prevfix)
                p.emit("pe", lambda e, ec=ec, hh=hh: e.matmul(
                    bank(hh), stA3[:, ec, :], qb3[:, ec, :],
                    start=(ec == 0), stop=(ec == 7)),
                    wait=w, inc=("pe" if ec == 7 else None),
                    tag=(f"q_A{q}_{hh}" if ec == 7 else None))
            for ec in range(8):
                p.emit("pe", lambda e, ec=ec, hh=hh: e.matmul(
                    bank(3 + hh, 0, 64), stB3[:, ec, :], qb3[:, ec, :],
                    start=(ec == 0), stop=(ec == 7)),
                    inc=("pe" if ec == 7 else None),
                    tag=(f"q_B{q}_{hh}" if ec == 7 else None))
            if q == 0 and hh == 0:
                for ec in range(8):
                    p.emit("pe", lambda e, ec=ec: e.matmul(
                        ps[:, 1024:1026], stA3[:, ec, :], extq[:, 2 * ec:2 * ec + 2],
                        start=(ec == 0), stop=(ec == 7)),
                        wait=(["extready"] if ec == 0 else []),
                        inc=("pe" if ec == 7 else None),
                        tag=("q_exA" if ec == 7 else None))
                for ec in range(8):
                    p.emit("pe", lambda e, ec=ec: e.matmul(
                        ps[0:64, 1026:1028], stB3[:, ec, :],
                        extq[:, 2 * ec:2 * ec + 2],
                        start=(ec == 0), stop=(ec == 7)),
                        inc=("pe" if ec == 7 else None),
                        tag=("q_exB" if ec == 7 else None))
                p.emit("dve", lambda e: e.tensor_copy(uqv[:, :], ps[:, 1024:1026]),
                       wait=["q_exA"])
                p.emit("dve", lambda e: e.tensor_copy(ukk[:, :], ps[0:64, 1026:1028]),
                       wait=["q_exB"], inc="dve", tag="q_ext")

            cs0 = 512 * hh
            b7 = ps[:, 3584:4096]
            b2f = ps[:, 1024:1536]
            wA = [f"q_A{q}_{hh}", f"q_repE{q}_1", "q_ext"]
            p.emit("dve", lambda e, hh=hh, cs0=cs0, b7=b7: e.scalar_tensor_tensor(
                b7, reps[:, 1024 + cs0:1024 + cs0 + 512], uqv[:, 0:1],
                bank(hh), op0=ALU.mult, op1=ALU.add),
                wait=wA)
            wA2 = [prevwB] if prevwB is not None else []
            p.emit("dve", lambda e, cs0=cs0, b7=b7, b2f=b2f: e.tensor_tensor(
                b2f, b7, reps[:, cs0:cs0 + 512], op=ALU.mult),
                wait=wA2, inc="dve", tag=f"q_fA{q}_{hh}")
            p.emit("dve", lambda e, cg=cg, b2f=b2f: e.tensor_scalar(
                qv[:, cg:cg + 512], b2f, uqv[:, 1:2], None, op0=ALU.add),
                wait=[f"q_fA{q}_{hh}"], inc="dve", tag=f"q_wA{q}_{hh}")
            p.emit("dve", lambda e, hh=hh, cs0=cs0, b7=b7: e.scalar_tensor_tensor(
                b7[0:64, :], reps[0:64, 1024 + cs0:1024 + cs0 + 512], ukk[:, 0:1],
                bank(3 + hh, 0, 64), op0=ALU.mult, op1=ALU.add),
                wait=[f"q_B{q}_{hh}"], inc="dve", tag=f"q_sB{q}_{hh}")
            p.emit("dve", lambda e, cs0=cs0, b7=b7, b2f=b2f: e.tensor_tensor(
                b2f[0:64, :], b7[0:64, :], reps[0:64, cs0:cs0 + 512], op=ALU.mult),
                wait=[f"q_wA{q}_{hh}"], inc="dve", tag=f"q_fB{q}_{hh}")
            p.emit("dve", lambda e, cg=cg, b2f=b2f: e.tensor_scalar(
                kk[0:64, cg:cg + 512], b2f[0:64, :], ukk[:, 1:2], None, op0=ALU.add),
                wait=[f"q_fB{q}_{hh}"], inc="dve", tag=f"q_wB{q}_{hh}")
            prevtag = f"q_B{q}_{hh}"
            prevfix = f"q_sB{q}_{hh}"
            prevwA = f"q_wA{q}_{hh}"
            prevwB = f"q_wB{q}_{hh}"

    if debug:
        p.emit("pool", lambda e: e.dma_start(dbg["qv"][:, :], qv[:, :]),
               wait=["q_wA3_1", "q_wB3_1"], inc="ds")
        p.emit("pool", lambda e: e.dma_start(dbg["kk"][:, :], kk[:, :]), inc="ds")

    for g in range(8):
        for u in range(4):
            c = 4 * g + u
            w = []
            if g == 0 and u == 0:
                w = ["q_wA3_1", "q_wB3_1", "ld_eyb", "q_repE3_1"]
            if u == 0 and g >= 2:
                w.append(f"vE{g-2}")
            bkc = 2560 + 512 * (g % 2)
            p.emit("pe", lambda e, c=c, u=u, bkc=bkc: e.matmul(
                ps[:, bkc + 64 * u:bkc + 64 * (u + 1)],
                qv[64:128, 128 * c:128 * (c + 1)], eyeb[64:128, 64:128],
                start=True, stop=True),
                wait=w, inc=("pe" if u == 3 else None),
                tag=(f"vT{g}" if u == 3 else None))
        p.emit("dve", lambda e, g=g, bkc=2560 + 512 * (g % 2): e.tensor_copy(
            vtok[:, 256 * g:256 * (g + 1)], ps[:, bkc:bkc + 256]),
            wait=[f"vT{g}"], inc="dve", tag=f"vE{g}")

    rowq = w2s[0:1, 0:T]
    qm2 = w2s[0:1, T:T + 1]
    kcols = rn2[:, 0:32]
    qm2c = rn2[:, 32:33]
    negall = rntok[:, 0:32]
    p.emit("dve", lambda e: e.tensor_tensor(
        a3[0:64, 1, :], kk[0:64, :], kk[0:64, :], op=ALU.mult),
        wait=["q_wB3_1", "cD_m15", "modm2_7"])
    p.emit("dve", lambda e: e.tensor_tensor(
        a3[0:64, 0, :], qv[0:64, :], qv[0:64, :], op=ALU.mult),
        wait=["q_wA3_1"], inc="dve", tag="nsq")
    for n in range(8):
        p.emit("pe", lambda e, n=n: e.matmul(
            ps[0:1, 512 * n:512 * (n + 1)], onec[0:64, 0:1],
            a3[0:64, 1, 512 * n:512 * (n + 1)], start=True, stop=True),
            wait=(["nsq", "vE7", "q_fB3_1", "q_ext"] if n == 0 else []),
            inc=("pe" if n == 7 else None), tag=("knT" if n == 7 else None))
    p.emit("dve", lambda e: e.tensor_copy(rowq, ps[0:1, 0:T]),
           wait=["knT"], inc="dve", tag="knE")
    for blk in range(32):
        w = ["knE"] if blk == 0 else []
        col = 2048 + 8 * (blk // 8) + (blk % 8)
        p.emit("pe", lambda e, blk=blk, col=col: e.matmul(
            ps[:, col:col + 1],
            rowq[0:1, 128 * blk:128 * (blk + 1)], eyeb[0:1, 0:1],
            start=True, stop=True),
            wait=w, inc=("pe" if blk == 31 else None),
            tag=("kcT" if blk == 31 else None))
    p.emit("dve", lambda e: e.tensor_copy(kcols, ps[:, 2048:2080]),
        wait=["kcT"], inc="dve", tag="kcE")
    for n in range(8):
        p.emit("pe", lambda e, n=n: e.matmul(
            ps[0:1, 512 * n:512 * (n + 1)], onec[0:64, 0:1],
            a3[0:64, 0, 512 * n:512 * (n + 1)], start=True, stop=True),
            wait=(["knE"] if n == 0 else []),
            inc=("pe" if n == 7 else None), tag=("qnT" if n == 7 else None))
    p.emit("dve", lambda e: e.tensor_copy(rowq, ps[0:1, 0:T]),
           wait=["qnT", "kcE"], inc="dve", tag="qnE")
    p.emit("dve", lambda e: e.reduce_max(qm2, rowq, axis=AX.X))
    p.emit("dve", lambda e: e.drain())
    p.emit("dve", lambda e: e.tensor_scalar(
        dmy[:, 0:1], dmy[:, 1:2], 1.0, None, op0=ALU.mult), inc="dve",
        tag="qmx")
    p.emit("pe", lambda e: e.matmul(ps[:, 2080:2081], oner[:, :], qm2,
                                    start=True, stop=True),
           wait=["qmx"], inc="pe", tag="qbT")
    p.emit("dve", lambda e: e.tensor_copy(qm2c, ps[:, 2080:2081]),
           wait=["qbT"], inc="dve", tag="qbE")
    p.emit("dve", lambda e: e.tensor_scalar(
        negall, kcols, qm2c, None, op0=ALU.mult))
    p.emit("dve", lambda e: e.tensor_scalar(
        dmy[:, 0:1], dmy[:, 1:2], 1.0, None, op0=ALU.mult))
    p.emit("dve", lambda e: e.tensor_scalar(
        dmy[:, 0:1], dmy[:, 1:2], 1.0, None, op0=ALU.mult), inc="dve",
        tag="mupre")
    p.emit("act", lambda e: e.activation(rn2[:, 0:32], negall, AF.Sqrt),
           wait=["mupre"], inc="act", tag="musq")
    p.emit("dve", lambda e: e.tensor_scalar(
        negall, rn2[:, 0:32], -1.01, None, op0=ALU.mult),
        wait=["musq"], inc="dve", tag="muall")
    p.emit("dve", lambda e: e.memset(kk[64:128, :], 0.0),
           wait=["cD_m15"], inc="dve", tag="kkz")

    p.wait("pe", ["vE7", "q_fB3_1", "q_ext", "muall", "qbE", "kkz"])
    def emit_av(c):
        cs = c % 2
        for nn in range(4):
            for cg2 in range(2):
                w = [f"vs{c}"]
                p.emit("pe", lambda e, c=c, cs=cs, nn=nn, cg2=cg2: e.matmul(
                    ps[64 * cg2:64 * (cg2 + 1), 512 * nn:512 * (nn + 1)],
                    vscb[:, 64 * cs:64 * (cs + 1)],
                    a3[:, cs, 2048 * cg2 + 512 * nn:2048 * cg2 + 512 * (nn + 1)],
                    start=(c == 0), stop=(c == NKC - 1),
                    tile_position=(0, 64 * cg2)),
                    wait=w, inc=("pe" if (nn == 3 and cg2 == 1) else None),
                    tag=(f"AV{c}" if (nn == 3 and cg2 == 1) else None))

    for c in range(NKC):
        cs = c % 2
        for s in range(4):
            gc = 2048 + 1024 * (s % 2)
            if s == 2 and c >= 1:
                emit_av(c - 1)
            for nn in range(2):
                i0 = 1024 * s + 512 * nn
                if c == 0 and s < 2:
                    w = []
                elif s >= 2:
                    w = [f"exp{c}_{s-2}"]
                else:
                    w = [f"exp{c-1}_{s+2}"]
                p.emit("pe", lambda e, c=c, i0=i0, gc=gc, nn=nn: e.matmul(
                    ps[:, gc + 512 * nn:gc + 512 * (nn + 1)],
                    kk[:, 128 * c:128 * (c + 1)], qv[:, i0:i0 + 512],
                    start=True, stop=True),
                    wait=w, inc=("pe" if nn == 1 else None),
                    tag=(f"S{c}_{s}" if nn == 1 else None))
            w = [f"S{c}_{s}", "muall"]
            if c >= 2:
                w += [f"AV{c-2}", f"vs{c-2}"]
            p.emit("act", lambda e, c=c, s=s, cs=cs, gc=gc: e.activation(
                a3[:, cs, 1024 * s:1024 * (s + 1)], ps[:, gc:gc + 1024], AF.Exp,
                bias=negall[:, c:c + 1],
                accum_out=zpart[:, 4 * cs + s:4 * cs + s + 1]),
                wait=w, inc="act", tag=f"exp{c}_{s}")
        p.emit("dve", lambda e, cs=cs: e.tensor_tensor(
            zval[:, cs:cs + 1], zpart[:, 4 * cs:4 * cs + 1],
            zpart[:, 4 * cs + 1:4 * cs + 2], op=ALU.add),
            wait=[f"exp{c}_{s2}" for s2 in range(4)])
        p.emit("dve", lambda e, cs=cs: e.tensor_tensor(
            zrec[:, cs:cs + 1], zpart[:, 4 * cs + 2:4 * cs + 3],
            zpart[:, 4 * cs + 3:4 * cs + 4], op=ALU.add))
        p.emit("dve", lambda e: e.tensor_scalar(
            dmy[:, 0:1], dmy[:, 1:2], 1.0, None, op0=ALU.mult))
        p.emit("dve", lambda e, cs=cs: e.tensor_tensor(
            zval[:, cs:cs + 1], zval[:, cs:cs + 1], zrec[:, cs:cs + 1],
            op=ALU.add))
        p.emit("dve", lambda e: e.drain())
        p.emit("dve", lambda e, cs=cs: e.reciprocal(zrec[:, cs:cs + 1],
                                                    zval[:, cs:cs + 1]))
        p.emit("dve", lambda e: e.drain())
        w = [f"vE{c//4}"]
        if c >= 2:
            w.append(f"AV{c-2}")
        p.emit("dve", lambda e, c=c, cs=cs: e.tensor_scalar(
            vscb[:, 64 * cs:64 * (cs + 1)], vtok[:, 64 * c:64 * (c + 1)],
            zrec[:, cs:cs + 1], None, op0=ALU.mult),
            wait=w, inc="dve", tag=f"vs{c}")
        if debug:
            p.emit("pool", lambda e, c=c, cs=cs: e.dma_start(
                dbg["z"][:, 2 * c:2 * c + 1], zval[:, cs:cs + 1],
                allow_slow_non_contiguous=True),
                wait=[f"vs{c}"], inc="ds")
            p.emit("pool", lambda e, c=c: e.dma_start(
                dbg["z"][:, 2 * c + 1:2 * c + 2], negall[:, c:c + 1],
                allow_slow_non_contiguous=True), inc="ds")
    emit_av(NKC - 1)

    p.emit("dve", lambda e: e.tensor_copy(avs[:, :], ps[:, 0:2048]),
           wait=[f"AV{NKC-1}"], inc="dve", tag="avE")
    for i, (dp, dc0, sp0) in enumerate([(0, 0, 0), (0, 2048, 64),
                                        (64, 0, 0), (64, 2048, 64)]):
        p.emit("pool", lambda e, dp=dp, dc0=dc0, sp0=sp0: e.dma_start(
            avd[dp:dp + 64, dc0:dc0 + 2048], avs[sp0:sp0 + 64, 0:2048]),
            wait=(["avE"] if i == 0 else []), inc="ds",
            tag=("avdup" if i == 3 else None))
    if debug:
        p.emit("pool", lambda e: e.dma_start(dbg["av"][:, :], avd[:, :]),
               wait=["avdup"], inc="ds")

    for q in range(4):
        for ec in range(8):
            slot = ec % 2
            if q == 0:
                p.emit("dve", lambda e, j=ec: e.scalar_tensor_tensor(
                    xfm3[:, j, :], avd[:, :], mvec[:, 16 + j:16 + j + 1],
                    xfm3[:, j, :], op0=ALU.mult, op1=ALU.add),
                    wait=(["avdup", "mva"] if ec == 0 else []),
                    inc="dve", tag=f"res1_{ec}")
                if debug:
                    p.emit("pool", lambda e, j=ec: e.dma_start(
                        dbg["x1"][:, T * j:T * (j + 1)], xfm3[:, j, :]),
                        wait=[f"res1_{ec}"], inc="ds")
            w = [f"res1_{ec}"]
            if q > 0 or ec >= 2:
                w.append(f"lnm{q}_{ec-2}" if ec >= 2 else f"lnm{q-1}_{6+ec}")
            p.emit("dve", lambda e, ec=ec, q=q, s=slot: e.tensor_copy(
                plbuf[:, 1024 * s:1024 * (s + 1)],
                xfm3[:, ec, 1024 * q:1024 * (q + 1)]),
                wait=w)
            p.emit("dve", lambda e, s=slot: e.tensor_tensor(
                sqb[:, 1024 * s:1024 * (s + 1)], plbuf[:, 1024 * s:1024 * (s + 1)],
                plbuf[:, 1024 * s:1024 * (s + 1)], op=ALU.mult),
                inc="dve", tag=f"lnc{q}_{ec}")
            for n in range(2):
                w = [f"lnc{q}_{ec}", "consts"]
                if q == 0 and ec == 0 and n == 0:
                    w.append("avE")
                if q > 0 and ec == 0 and n == 0:
                    w.append(f"lnrowE{q-1}")
                p.emit("pe", lambda e, s=slot, n=n, ec=ec: e.matmul(
                    ps[0:1, 512 * n:512 * (n + 1)], onec[:, 0:1],
                    plbuf[:, 1024 * s + 512 * n:1024 * s + 512 * (n + 1)],
                    start=(ec == 0), stop=(ec == 7)),
                    wait=w)
                p.emit("pe", lambda e, s=slot, n=n, ec=ec: e.matmul(
                    ps[0:1, 1024 + 512 * n:1024 + 512 * (n + 1)], onec[:, 0:1],
                    sqb[:, 1024 * s + 512 * n:1024 * s + 512 * (n + 1)],
                    start=(ec == 0), stop=(ec == 7)),
                    inc=("pe" if n == 1 else None),
                    tag=(f"lnm{q}_{ec}" if n == 1 else None))
        p.emit("dve", lambda e: e.tensor_copy(ln2rows, ps[0:1, 0:2048]),
               wait=[f"lnm{q}_7"], inc="dve", tag=f"lnrowE{q}")
        for u in range(16):
            w = []
            if u == 0:
                w = [f"lnrowE{q}"]
                if q > 0:
                    w.append(f"lncolE{q-1}")
            p.emit("pe", lambda e, u=u: e.matmul(
                ps[:, 2048 + u:2048 + u + 1], ln2rows[0:1, 128 * u:128 * (u + 1)],
                eyeb[0:1, 0:1], start=True, stop=True),
                wait=w, inc=("pe" if u == 15 else None),
                tag=(f"lncT{q}" if u == 15 else None))
        p.emit("dve", lambda e: e.tensor_copy(colsb[:, :], ps[:, 2048:2064]),
               wait=[f"lncT{q}"], inc="dve", tag=f"lncolE{q}")
        mq = colsb[:, 0:8]
        sq_ = colsb[:, 8:16]
        p.emit("dve", lambda e, mq=mq: e.tensor_scalar(mq, mq, 1.0 / E, None,
                                                       op0=ALU.mult))
        p.emit("dve", lambda e, sq_=sq_: e.tensor_scalar(sq_, sq_, 1.0 / E, None,
                                                         op0=ALU.mult))
        p.emit("dve", lambda e, q=q, mq=mq: e.tensor_copy(
            rn2[:, 16 * q:16 * q + 16:2], mq))
        p.emit("dve", lambda e, mq=mq: e.tensor_tensor(mq, mq, mq, op=ALU.mult))
        p.emit("dve", lambda e: e.tensor_scalar(
            dmy[:, 0:1], dmy[:, 1:2], 1.0, None, op0=ALU.mult))
        p.emit("dve", lambda e, mq=mq, sq_=sq_: e.tensor_tensor(
            sq_, sq_, mq, op=ALU.subtract))
        p.emit("dve", lambda e: e.tensor_scalar(
            dmy[:, 0:1], dmy[:, 1:2], 1.0, None, op0=ALU.mult))
        p.emit("dve", lambda e, sq_=sq_: e.tensor_scalar(sq_, sq_, EPS, None,
                                                         op0=ALU.add))
        p.emit("dve", lambda e: e.drain())
        p.emit("dve", lambda e, q=q, sq_=sq_: e.reciprocal(
            rn2[:, 16 * q + 1:16 * q + 16:2], sq_))
        p.emit("dve", lambda e: e.drain())
        p.emit("dve", lambda e: e.tensor_scalar(
            dmy[:, 0:1], dmy[:, 1:2], 1.0, None, op0=ALU.mult),
            inc="dve", tag=f"lnv{q}")
    p.emit("act", lambda e: e.activation(rn2[:, 1:64:2], rn2[:, 1:64:2], AF.Sqrt),
           wait=[f"lnv{qq}" for qq in range(4)], inc="act", tag="ln2r")
    p.emit("dve", lambda e: e.tensor_tensor(
        hstat[:, 64:96], rn2[:, 0:64:2], rn2[:, 1:64:2], op=ALU.mult),
        wait=["ln2r"])
    p.emit("dve", lambda e: e.tensor_scalar(
        dmy[:, 0:1], dmy[:, 1:2], 1.0, None, op0=ALU.mult))
    p.emit("dve", lambda e: e.tensor_scalar(
        dmy[:, 0:1], dmy[:, 1:2], 1.0, None, op0=ALU.mult))
    p.emit("dve", lambda e: e.tensor_scalar(
        rn2[:, 0:64:2], hstat[:, 64:96], -1.0, None, op0=ALU.mult),
        inc="dve", tag="ln2nm")

    for dc in range(8):
        for ec in range(8):
            w = []
            if dc == 0 and ec == 0:
                w = ["wcready", "extready", "lnm3_7", "lnrowE3"]
            p.emit("pe", lambda e, dc=dc, ec=ec: e.matmul(
                ps[:, 2 * dc:2 * dc + 2], w3v[:, ec, 128 * dc:128 * (dc + 1)],
                extf[:, 2 * ec:2 * ec + 2], start=(ec == 0), stop=(ec == 7)),
                wait=w, inc=("pe" if (ec == 7 and dc == 7) else None),
                tag=("fext" if (ec == 7 and dc == 7) else None))
    p.emit("dve", lambda e: e.tensor_copy(u3b[:, :], ps[:, 0:16]),
           wait=["fext"], inc="dve", tag="fextE")
    p.emit("dve", lambda e: e.tensor_scalar(
        dmy[:, 0:1], dmy[:, 1:2], 1.0, None, op0=ALU.mult))
    p.emit("dve", lambda e: e.tensor_scalar(
        dmy[:, 0:1], dmy[:, 1:2], 1.0, None, op0=ALU.mult))
    p.emit("dve", lambda e: e.tensor_tensor(bias3[:, :], u3b[:, 1:16:2], fbf[:, :],
                                            op=ALU.add),
           wait=["fextE", "fbE"], inc="dve", tag="bias3r")
    if debug:
        p.emit("pool", lambda e: e.dma_start(dbg["u3"][:, 0:16], u3b[:, :]),
               wait=["fextE"], inc="ds")
        p.emit("pool", lambda e: e.dma_start(dbg["u3"][:, 16:24], bias3[:, :]),
               wait=["bias3r"], inc="ds")

    for q in range(4):
        for r in range(2):
            roff = 2048 + 1024 * r
            for u in range(8):
                tq = 16 * q + 2 * u + (1 - r)
                w = []
                if u == 0:
                    w = ["ln2nm"]
                    if r == 0:
                        w.append("fextE" if q == 0 else f"tanh{q-1}_7")
                    if r == 0 and q > 0:
                        w += [f"oE{8*q-2}", f"oEa{8*q-2}"]
                p.emit("pe", lambda e, u=u, tq=tq, roff=roff: e.transpose(
                    ps[0:1, roff + 128 * u:roff + 128 * (u + 1)],
                    rn2[:, tq:tq + 1], eyef[:, :]),
                    wait=w, inc=("pe" if u == 7 else None),
                    tag=(f"f_rT{q}_{r}" if u == 7 else None))
            p.emit("dve", lambda e, r=r, roff=roff: e.tensor_copy(
                rowbf[0:1, 1024 * r:1024 * (r + 1)], ps[0:1, roff:roff + 1024]),
                wait=[f"f_rT{q}_{r}"], inc="dve", tag=f"f_rE{q}_{r}")
        for r in range(2):
            roff = 2048 + 1024 * r
            for n in range(2):
                p.emit("pe", lambda e, r=r, n=n, roff=roff: e.matmul(
                    ps[:, roff + 512 * n:roff + 512 * (n + 1)], oner[:, :],
                    rowbf[0:1, 1024 * r + 512 * n:1024 * r + 512 * (n + 1)],
                    start=True, stop=True),
                    wait=([f"f_rE{q}_1"] if (r == 0 and n == 0) else []),
                    inc=("pe" if n == 1 else None),
                    tag=(f"f_bT{q}_{r}" if n == 1 else None))
            p.emit("dve", lambda e, r=r, roff=roff: e.tensor_copy(
                reps[:, 1024 * r:1024 * (r + 1)], ps[:, roff:roff + 1024]),
                wait=[f"f_bT{q}_{r}"], inc="dve", tag=f"f_repE{q}_{r}")
        for ec in range(8):
            wsc = []
            if ec == 0:
                wsc = [f"f_repE{q}_1"]
                if q > 0:
                    wsc.append(f"fP{q-1}_7")
            p.emit("dve", lambda e, ec=ec, q=q: e.tensor_tensor(
                xs3[:, ec, :], xfm3[:, ec, 1024 * q:1024 * (q + 1)],
                reps[:, 0:1024], op=ALU.mult),
                wait=wsc, inc=("dve" if ec == 7 else None),
                tag=(f"f_sc{q}" if ec == 7 else None))
        for dc in range(8):
            pb = dc % 2
            pc0 = 1024 * pb
            t20 = 2048 + 1024 * pb
            for ec in range(8):
                for n in range(2):
                    w = []
                    if ec == 0 and n == 0:
                        w = [f"f_sc{q}"]
                        if dc >= 2:
                            w.append(f"tanh{q}_{dc-2}")
                        elif q > 0:
                            w.append(f"tanh{q-1}_{6+dc}")
                            w.append(f"oE{8*q-3}")
                            w.append(f"oEa{8*q-3}")
                        if q == 0 and dc <= 1:
                            w.append("fextE")
                        if dc <= 1:
                            w.append(f"f_repE{q}_1")
                    p.emit("pe", lambda e, dc=dc, ec=ec, n=n, pc0=pc0: e.matmul(
                        ps[:, pc0 + 512 * n:pc0 + 512 * (n + 1)],
                        w3v[:, ec, 128 * dc:128 * (dc + 1)],
                        xs3[:, ec, 512 * n:512 * (n + 1)],
                        start=(ec == 0), stop=(ec == 7)),
                        wait=w, inc=("pe" if (ec == 7 and n == 1) else None),
                        tag=(f"fP{q}_{dc}" if (ec == 7 and n == 1) else None))
            p.emit("dve", lambda e, pc0=pc0, t20=t20, dc=dc: e.scalar_tensor_tensor(
                ps[:, t20:t20 + 1024], reps[:, 1024:2048], u3b[:, 2 * dc:2 * dc + 1],
                ps[:, pc0:pc0 + 1024], op0=ALU.mult, op1=ALU.add),
                wait=[f"fP{q}_{dc}", f"f_repE{q}_1", "bias3r"],
                inc="dve", tag=f"fT2{q}_{dc}")
            wth = [f"fT2{q}_{dc}"]
            if dc >= 2:
                wth.append(f"res2_{q}_{dc-2}")
            elif q > 0:
                wth.append(f"res2_{q-1}_{6+dc}")
            elif dc == 0:
                wth.append("avdup")
                wth.append(f"res1_{NCH-1}")
            p.emit("act", lambda e, dc=dc, t20=t20: e.activation(
                tnh[:, 1024 * (dc % 2):1024 * (dc % 2) + 1024], ps[:, t20:t20 + 1024],
                AF.Tanh, bias=bias3[:, dc:dc + 1]),
                wait=wth, inc="act", tag=f"tanh{q}_{dc}")
            p.emit("dve", lambda e, dc=dc, q=q: e.scalar_tensor_tensor(
                xfm3[:, dc, 1024 * q:1024 * (q + 1)],
                tnh[:, 1024 * (dc % 2):1024 * (dc % 2) + 1024],
                mvec[:, 40 + dc:40 + dc + 1], xfm3[:, dc, 1024 * q:1024 * (q + 1)],
                op0=ALU.mult, op1=ALU.add),
                wait=[f"tanh{q}_{dc}", "mvb"], inc="dve", tag=f"res2_{q}_{dc}")

        for ti in range(8):
            t = 8 * q + ti
            b0c = 1024 * (ti % 4)
            for j in range(NCH):
                w = []
                if j == 0:
                    if ti == 0:
                        w = [f"fT2{q}_7", f"tanh{q}_7"] + \
                            [f"res2_{q}_{k}" for k in range(NCH)]
                    if ti >= 4:
                        w = [f"oE{t-4}", f"oEa{t-4}"]
                col = b0c + 512 * (j // 4) + 128 * (j % 4)
                p.emit("pe", lambda e, t=t, j=j, col=col: e.transpose(
                    ps[:, col:col + 128], xfm3[:, j, 128 * t:128 * (t + 1)],
                    eyef[:, :]),
                    wait=w, inc=("pe" if j == NCH - 1 else None),
                    tag=(f"oT{t}" if j == NCH - 1 else None))
            p.emit("dve", lambda e, t=t, b0c=b0c: e.tensor_copy(
                xfm3[:, 0:4, 128 * t:128 * t + 128], ps[:, b0c:b0c + 512]),
                wait=[f"oT{t}"], inc="dve", tag=f"oE{t}")
            p.emit("act", lambda e, t=t, b0c=b0c: e.copy(
                xfm3[:, 4:8, 128 * t:128 * t + 128],
                ps[:, b0c + 512:b0c + 1024]),
                wait=[f"oT{t}"], inc="act", tag=f"oEa{t}")
            p.emit("sp", lambda e, t=t: e.dma_start(
                out_h[128 * t:128 * (t + 1), :].rearrange("p (c m) -> p c m",
                                                          m=128),
                xfm3[:, :, 128 * t:128 * (t + 1)]),
                wait=[f"oE{t}", f"oEa{t}"], inc="do", tag=f"ost{t}")

    p.wait("sp", [f"ost{t}" for t in range(NT)])

    p.lower(nc, semh)
    es.close()
    return nc


_NC_CACHE = {}


def _get_nc(debug=False):
    if debug not in _NC_CACHE:
        _NC_CACHE[debug] = build(debug)
    return _NC_CACHE[debug]


def make_in_maps(x, wq, wk, wv, w1, w2, w3, fb, c, wm):
    eye = np.eye(128, dtype=np.float32)
    eyb = np.eye(128).astype(ml_dtypes.bfloat16)
    in_maps = []
    for b in range(B):
        in_maps.append({
            "x": x[b], "wq": wq, "wk": wk, "wv": wv,
            "w1": w1, "w2": w2, "w3": w3, "fbias": fb,
            "crow": c[b:b + 1], "wmod": wm, "eye": eye, "eyeb": eyb,
        })
    return in_maps


def kernel(L0_attn_x, L0_attn_Wq, L0_attn_Wk, L0_attn_Wv, L0_ffn_weight,
           L0_ffn_aux_weight1, L0_ffn_aux_weight2, L0_ffn_bias, L0_c, L0_wmod):
    f = lambda a: np.ascontiguousarray(np.asarray(a, dtype=np.float32))
    x = f(L0_attn_x)
    in_maps = make_in_maps(x, f(L0_attn_Wq), f(L0_attn_Wk), f(L0_attn_Wv),
                           f(L0_ffn_weight), f(L0_ffn_aux_weight1),
                           f(L0_ffn_aux_weight2), f(L0_ffn_bias).reshape(1, E),
                           f(L0_c), f(L0_wmod))
    nc = _get_nc(False)
    res = run_bass_kernel_spmd(nc, in_maps, core_ids=list(range(B)))
    return np.stack([res.results[b]["out"] for b in range(B)], axis=0)


# version 12
# speedup vs baseline: 1.0282x; 1.0282x over previous
import numpy as np
import ml_dtypes
import concourse.bass as bass
import concourse.mybir as mybir
from contextlib import ExitStack
from concourse.bass_utils import run_bass_kernel_spmd

F32 = mybir.dt.float32
BF16 = mybir.dt.bfloat16
U8 = mybir.dt.uint8
ALU = mybir.AluOpType
AF = mybir.ActivationFunctionType
AX = mybir.AxisListType

B, T, E, D = 8, 4096, 1024, 64
NCH = E // 128
NT = T // 128
NKC = T // 128
EPS = 1e-5

ENGS = ("pe", "act", "dve", "pool", "sp")
DSEMS = tuple(f"dx{t}" for t in range(32)) + ("dwm0", "dwm1", "dfb", "deye", "deyb", "dc", "dw1", "dw2", "dw3", "dqk", "do", "ds")


class Prog:
    def __init__(self):
        self.stream = {e: [] for e in ENGS}
        self.cnt = {s: 0 for s in ENGS + DSEMS}
        self.mile = {}
        self.lastwait = {}

    def _waits(self, eng, wait):
        ws = []
        for w in wait:
            if w is None:
                continue
            sem, val, peng = self.mile[w] if isinstance(w, str) else w
            if peng == eng and sem in ENGS:
                continue
            key = (eng, sem)
            if self.lastwait.get(key, -1) >= val:
                continue
            self.lastwait[key] = val
            ws.append((sem, val))
        return ws

    def emit(self, eng, fn, wait=(), inc=None, tag=None):
        ws = self._waits(eng, wait)
        amt = 0
        if inc is not None:
            amt = 16 if inc in DSEMS else 1
            self.cnt[inc] += amt
            if tag is not None:
                self.mile[tag] = (inc, self.cnt[inc], eng)
        elif tag is not None:
            raise ValueError("tag requires inc")

        def closure(e, sems):
            for sem, val in ws:
                e.wait_ge(sems[sem], val)
            inst = fn(e)
            if inc is not None:
                inst.then_inc(sems[inc], amt)
        self.stream[eng].append(closure)

    def wait(self, eng, wait):
        ws = self._waits(eng, wait)
        if not ws:
            return

        def closure(e, sems):
            for sem, val in ws:
                e.wait_ge(sems[sem], val)
        self.stream[eng].append(closure)

    def lower(self, nc, sems):
        with nc.Block() as block:
            for eng, deco in (("pe", block.tensor), ("act", block.scalar),
                              ("dve", block.vector), ("pool", block.gpsimd),
                              ("sp", block.sync)):
                stream = self.stream[eng]

                def run(e, stream=stream):
                    for closure in stream:
                        closure(e, sems)
                deco(run)


def build(debug=False):
    nc = bass.Bass("TRN2", target_bir_lowering=False)

    x_hbm = nc.dram_tensor("x", [T, E], F32, kind="ExternalInput")
    wq_h = nc.dram_tensor("wq", [E, D], F32, kind="ExternalInput")
    wk_h = nc.dram_tensor("wk", [E, D], F32, kind="ExternalInput")
    wv_h = nc.dram_tensor("wv", [E, D], F32, kind="ExternalInput")
    w1_h = nc.dram_tensor("w1", [E, E], F32, kind="ExternalInput")
    w2_h = nc.dram_tensor("w2", [E, E], F32, kind="ExternalInput")
    w3_h = nc.dram_tensor("w3", [E, E], F32, kind="ExternalInput")
    fb_h = nc.dram_tensor("fbias", [1, E], F32, kind="ExternalInput")
    c_h = nc.dram_tensor("crow", [1, E], F32, kind="ExternalInput")
    wm_h = nc.dram_tensor("wmod", [E, 6 * E], F32, kind="ExternalInput")
    eye_h = nc.dram_tensor("eye", [128, 128], F32, kind="ExternalInput")
    eyb_h = nc.dram_tensor("eyeb", [128, 128], BF16, kind="ExternalInput")
    out_h = nc.dram_tensor("out", [T, E], F32, kind="ExternalOutput")
    dbg = {}
    if debug:
        dbg["mvec"] = nc.dram_tensor("d_mvec", [128, 96], F32, kind="ExternalOutput")
        dbg["qv"] = nc.dram_tensor("d_qv", [128, T], F32, kind="ExternalOutput")
        dbg["kk"] = nc.dram_tensor("d_kk", [128, T], F32, kind="ExternalOutput")
        dbg["stats"] = nc.dram_tensor("d_stats", [128, 128], F32, kind="ExternalOutput")
        dbg["z"] = nc.dram_tensor("d_z", [128, 64], F32, kind="ExternalOutput")
        dbg["av"] = nc.dram_tensor("d_av", [128, T], F32, kind="ExternalOutput")
        dbg["x1"] = nc.dram_tensor("d_x1", [128, NCH * T], F32, kind="ExternalOutput")
        dbg["wc"] = nc.dram_tensor("d_wc", [128, NCH * E], F32, kind="ExternalOutput")
        dbg["u3"] = nc.dram_tensor("d_u3", [128, 32], F32, kind="ExternalOutput")
        dbg["xf"] = nc.dram_tensor("d_xf", [128, NCH * T], F32, kind="ExternalOutput")
        dbg["hs"] = nc.dram_tensor("d_hs", [128, 128], F32, kind="ExternalOutput")
        dbg["ln1"] = nc.dram_tensor("d_ln1", [128, 64], F32, kind="ExternalOutput")

    es = ExitStack()
    sb = lambda n, sh, dt=F32: es.enter_context(nc.sbuf_tensor(n, sh, dt))

    xfm = sb("xfm", [128, NCH * T])
    fbrow = sb("fbrow", [1, E])
    hstat = sb("hstat", [128, 128])
    rntok = sb("rntok", [128, 64])
    rn2 = sb("rn2", [128, 64])
    modf = sb("modf", [128, 48])
    mvec = sb("mvec", [128, 48])
    colsb = sb("colsb", [128, 16])
    uqv = sb("uqv", [128, 2])
    ukk = sb("ukk", [64, 2])
    u3b = sb("u3b", [128, 16])
    bias3 = sb("bias3", [128, 8])
    fbf = sb("fbf", [128, 8])
    negmu = sb("negmu", [128, 2])
    zpart = sb("zpart", [128, 8])
    zval = sb("zval", [128, 2])
    zrec = sb("zrec", [128, 2])
    dmy = sb("dmy", [128, 2])
    eyef = sb("eyef", [128, 128])

    w1s = sb("w1s", [128, NCH * E], BF16)
    w2s = sb("w2s", [128, NCH * E], BF16)
    w3s = sb("w3s", [128, NCH * E], BF16)
    m1s = sb("m1s", [128, NCH * E], BF16)
    scrb = sb("scrb", [128, 2048], BF16)
    vscb = sb("vscb", [128, 2 * D], BF16)
    stA = sb("stA", [128, NCH * 128], BF16)
    stB = sb("stB", [128, NCH * 64], BF16)
    extq = sb("extq", [128, 16], BF16)
    extf = sb("extf", [128, 16], BF16)
    onec = sb("onec", [128, 2], BF16)
    oner = sb("oner", [1, 128], BF16)
    csfm = sb("csfm", [128, 8], BF16)
    eyeb = sb("eyebs", [128, 128], BF16)
    gmask = sb("gmask", [128, 16], U8)

    ps = es.enter_context(nc.psum_tensor("ps", [128, 4096], F32))

    semh = {}
    for s in ENGS + DSEMS:
        semh[s] = es.enter_context(nc.semaphore("sem_" + s))

    p = Prog()

    xfm3 = xfm[:, :].rearrange("p (c m) -> p c m", m=T)
    w1v = w1s[:, :].rearrange("p (c m) -> p c m", m=E)
    w2v = w2s[:, :].rearrange("p (c m) -> p c m", m=E)
    w3v = w3s[:, :].rearrange("p (c m) -> p c m", m=E)
    m1v = m1s[:, :].rearrange("p (c m) -> p c m", m=E)
    stA3 = stA[:, :].rearrange("p (c m) -> p c m", m=128)
    stB3 = stB[:, :].rearrange("p (c m) -> p c m", m=64)

    qv = w1s[:, 0:T]
    kk = w1s[:, T:2 * T]
    qb3 = w2s[:, 0:4096].rearrange("p (c m) -> p c m", m=512)
    avs = w2s[:, 0:2048]
    avd = w2s[:, 4096:8192]
    plbuf = w2s[:, 0:2048]
    sqb = w2s[:, 2048:4096]
    tnh = w2s[:, 4096:6144]
    a3 = m1s[:, :].rearrange("p (c m) -> p c m", m=T)
    xs3 = m1s[:, :].rearrange("p (c m) -> p c m", m=1024)
    csrow = scrb[0:1, 0:E]
    reps = scrb[:, 0:2048]
    vtok = scrb[:, 0:2048]
    wmbuf = m1s[:, 4096:8192]
    modrow = fbrow[0:1, 0:1024]
    rowbf = m1s[0:1, 0:2048]
    ln2rows = scrb[0:1, 0:2048]
    jitb = stA[:, 0:1024]
    bnb = colsb[:, 0:12]

    def bank(i, lo=0, hi=128):
        return ps[lo:hi, 512 * i:512 * (i + 1)]

    p.emit("sp", lambda e: e.dma_start(fbrow[:, :], fb_h[:, :]), inc="dfb", tag="ld_fb")
    p.emit("sp", lambda e: e.dma_start(eyef[:, :], eye_h[:, :]), inc="deye", tag="ld_eye")
    p.emit("sp", lambda e: e.dma_start(eyeb[:, :], eyb_h[:, :]), inc="deyb", tag="ld_eyb")

    p.emit("pool", lambda e: e.dma_start(csrow, c_h[:, :]), inc="dc", tag="ld_c")
    p.emit("pool", lambda e: e.dma_start(
        w2v[:, :, :], w2_h.ap().rearrange("(c p) d -> p c d", p=128)),
        inc="dw2", tag="ld_w2")
    p.emit("pool", lambda e: e.dma_start(
        w3v[:, :, :], w3_h.ap().rearrange("(c p) d -> p c d", p=128)),
        inc="dw3", tag="ld_w3")
    p.emit("pool", lambda e: e.dma_start(
        w1v[:, :, :], w1_h.ap().rearrange("(c p) d -> p c d", p=128)),
        inc="dw1", tag="ld_w1")

    for t in range(NT):
        p.emit("sp", lambda e, t=t: e.dma_start(
            xfm3[:, :, 128 * t:128 * (t + 1)],
            x_hbm[128 * t:128 * (t + 1), :].rearrange("p (c m) -> p c m", m=128)),
            inc=f"dx{t}", tag=f"xld{t}")

    p.emit("dve", lambda e: e.memset(onec[:, :], 1.0))
    p.emit("dve", lambda e: e.memset(dmy[:, :], 1.0))
    p.emit("dve", lambda e: e.memset(oner[:, :], 1.0), inc="dve", tag="consts")

    jslots = (jitb, scrb[:, 1024:2048])
    def mul_jit(src3, rhs3, dst3, first_wait, tg):
        nt = 0
        for oc in range(8):
            jb = jslots[oc % 2]
            for k in range(8):
                w = []
                if k == 0:
                    w = list(first_wait) if oc == 0 else []
                    if oc >= 1:
                        w.append(f"{tg}_je{oc-1}")
                col = 1024 + 512 * (k // 4) + 128 * (k % 4)
                p.emit("pe", lambda e, s=src3, oc=oc, k=k, col=col: e.matmul(
                    ps[:, col:col + 128], s[:, oc, 128 * k:128 * (k + 1)],
                    eyeb[:, :], start=True, stop=True),
                    wait=w, inc=("pe" if k == 7 else None),
                    tag=(f"{tg}_jt{oc}" if k == 7 else None))
            for g in range(2):
                w = [f"{tg}_jt{oc}"]
                if g == 0 and nt >= 4:
                    w.append(f"{tg}_m{nt-3}")
                p.emit("act", lambda e, g=g, jb=jb: e.copy(
                    jb[:, 512 * g:512 * (g + 1)],
                    ps[:, 512 * (2 + g):512 * (2 + g) + 512]),
                    wait=w, inc=("act" if g == 1 else None),
                    tag=(f"{tg}_je{oc}" if g == 1 else None))
            for n in range(2):
                bk = 4 + nt % 2
                for k in range(8):
                    w = []
                    if k == 0:
                        w.append(f"{tg}_je{oc}")
                        if nt >= 2:
                            w.append(f"{tg}_e{nt-2}")
                        if oc == 0 and n == 0:
                            w += list(first_wait)
                    p.emit("pe", lambda e, r=rhs3, oc=oc, n=n, k=k, bk=bk, jb=jb:
                           e.matmul(bank(bk), jb[:, 128 * k:128 * (k + 1)],
                                    r[:, k, 512 * n:512 * (n + 1)],
                                    start=(k == 0), stop=(k == 7)),
                           wait=w, inc=("pe" if k == 7 else None),
                           tag=(f"{tg}_m{nt}" if k == 7 else None))
                p.emit("act", lambda e, d=dst3, oc=oc, n=n, bk=bk: e.copy(
                    d[:, oc, 512 * n:512 * (n + 1)], bank(bk)),
                    wait=[f"{tg}_m{nt}"], inc="act", tag=f"{tg}_e{nt}")
                nt += 1
            yield
        p.wait("pe", [f"{tg}_e{nt-1}"])

    cb_gen = mul_jit(w2v, w3v, m1v, ["ld_w2", "ld_w3", "ld_eyb"], "cB")
    cd_gen = mul_jit(w1v, m1v, w3v, ["ld_w1", "cB_e15"], "cD")
    CBSET = {12, 14, 16, 18, 20, 22, 24, 26}
    CDSET = {28, 30}

    for t in range(NT):
        for j in range(NCH):
            p.emit("dve", lambda e, t=t, j=j: e.bn_stats(
                rn2[:, 6 * j:6 * j + 6], xfm3[:, j, 128 * t:128 * (t + 1)]),
                wait=([f"xld{t}"] if j == 0 else []))
        p.emit("dve", lambda e: e.drain())
        me = rn2[:, 1:48:6]
        mo = rn2[:, 4:48:6]
        Me = rn2[:, 2:48:6]
        Mo = rn2[:, 5:48:6]
        h96 = hstat[:, 96:104]
        h104 = hstat[:, 104:112]
        h112 = hstat[:, 112:120]
        h120 = hstat[:, 120:128]
        p.emit("dve", lambda e, me=me: e.tensor_tensor(h104, me, me, op=ALU.mult))
        p.emit("dve", lambda e, mo=mo: e.tensor_tensor(h112, mo, mo, op=ALU.mult))
        p.emit("dve", lambda e, me=me, mo=mo: e.tensor_tensor(h96, me, mo,
                                                              op=ALU.add))
        p.emit("dve", lambda e: e.tensor_tensor(h104, h104, h112, op=ALU.add))
        p.emit("dve", lambda e, Me=Me, Mo=Mo: e.tensor_tensor(h120, Me, Mo,
                                                              op=ALU.add))
        bases = ((96, 3 * t), (104, 3 * t + 1), (120, 3 * t + 2))
        for b_, _ in bases:
            hb = hstat[:, b_:b_ + 8]
            p.emit("dve", lambda e, hb=hb: e.tensor_tensor(
                hb[:, 0:4], hb[:, 0:4], hb[:, 4:8], op=ALU.add))
        for b_, _ in bases:
            hb = hstat[:, b_:b_ + 8]
            p.emit("dve", lambda e, hb=hb: e.tensor_tensor(
                hb[:, 0:2], hb[:, 0:2], hb[:, 2:4], op=ALU.add))
        for b_, dstc in bases:
            hb = hstat[:, b_:b_ + 8]
            p.emit("dve", lambda e, hb=hb, dstc=dstc: e.tensor_tensor(
                hstat[:, dstc:dstc + 1], hb[:, 0:1], hb[:, 1:2], op=ALU.add),
                inc=("dve" if b_ == 120 else None),
                tag=(f"xbn{t}" if b_ == 120 else None))
        for j in range(NCH):
            w = [f"xld{t}", "ld_eye", f"xbn{t}"]
            if j == 0 and t >= 1:
                w.append(f"xE{t-1}")
            col = 512 * (j // 4) + 128 * (j % 4)
            p.emit("pe", lambda e, t=t, j=j, c=col: e.transpose(
                ps[:, c:c + 128], xfm3[:, j, 128 * t:128 * (t + 1)], eyef[:, :]),
                wait=w, inc=("pe" if j == NCH - 1 else None),
                tag=(f"xT{t}" if j == NCH - 1 else None))
        for g in range(2):
            dst = xfm3[:, 4 * g:4 * g + 4, 128 * t:128 * t + 128]
            p.emit("act", lambda e, d=dst, c=512 * g: e.copy(
                d, ps[:, c:c + 512]),
                wait=[f"xT{t}"], inc=("act" if g == 1 else None),
                tag=(f"xE{t}" if g == 1 else None))
        if t in CBSET:
            next(cb_gen)
        if t in CDSET:
            next(cd_gen)

    msu = hstat[:, 0:96:3]
    m2u = hstat[:, 1:96:3]
    M2u = hstat[:, 2:96:3]
    s_, d_ = rn2[:, 0:32], rn2[:, 32:64]
    ex2 = hstat[:, 96:128]
    p.emit("dve", lambda e: e.drain(), wait=[f"xbn{NT-1}"])
    p.emit("dve", lambda e, m2u=m2u: e.tensor_scalar(
        ex2, m2u, 1.0 / 16, None, op0=ALU.mult))
    p.emit("dve", lambda e, msu=msu: e.tensor_scalar(
        s_, msu, 1.0 / 16, None, op0=ALU.mult))
    p.emit("dve", lambda e, M2u=M2u: e.scalar_tensor_tensor(
        ex2, M2u, 1.0 / 1024, ex2, op0=ALU.mult, op1=ALU.add))
    p.emit("dve", lambda e: e.tensor_tensor(d_, s_, s_, op=ALU.mult))
    p.emit("dve", lambda e: e.tensor_scalar(
        dmy[:, 0:1], dmy[:, 1:2], 1.0, None, op0=ALU.mult))
    p.emit("dve", lambda e: e.tensor_tensor(d_, ex2, d_, op=ALU.subtract))
    p.emit("dve", lambda e: e.tensor_scalar(
        dmy[:, 0:1], dmy[:, 1:2], 1.0, None, op0=ALU.mult))
    p.emit("dve", lambda e: e.tensor_scalar(d_, d_, EPS, None, op0=ALU.add))
    p.emit("dve", lambda e: e.drain())
    p.emit("dve", lambda e: e.reciprocal(d_, d_))
    p.emit("dve", lambda e: e.drain())
    p.emit("dve", lambda e: e.tensor_scalar(
        dmy[:, 0:1], dmy[:, 1:2], 1.0, None, op0=ALU.mult), inc="dve",
        tag="ln1v")
    p.emit("act", lambda e: e.activation(rntok[:, 0:64:2], d_, AF.Sqrt),
           wait=["ln1v"], inc="act", tag="ln1r")
    p.emit("dve", lambda e: e.tensor_scalar(
        rntok[:, 1:64:2], s_, -1.0, None, op0=ALU.mult),
        wait=["ln1r"], inc="dve", tag="ln1done")
    if debug:
        p.emit("pool", lambda e: e.dma_start(dbg["stats"][:, 0:64], rntok[:, :]),
               wait=["ln1done"], inc="ds")
        p.emit("pool", lambda e: e.dma_start(dbg["hs"][:, :], hstat[:, :]),
               wait=[f"xbn{NT-1}"], inc="ds")
        p.emit("pool", lambda e: e.dma_start(dbg["ln1"][:, :], rn2[:, 0:64]),
               wait=["ln1v"], inc="ds")
        p.emit("pool", lambda e: e.dma_start(dbg["xf"][:, :], xfm[:, :]),
               wait=[f"xE{NT-1}", f"xbn{NT-1}"], inc="ds")

    for _ in cb_gen:
        pass
    for _ in cd_gen:
        pass

    for j in range(8):
        p.emit("pe", lambda e, j=j: e.transpose(
            ps[:, 3080 + j:3080 + j + 1], fbrow[0:1, 128 * j:128 * (j + 1)],
            eyef[0:1, 0:1]),
            wait=(["ld_fb", "ld_eye", "cD_e15"] if j == 0 else []),
            inc=("pe" if j == 7 else None), tag=("fbT" if j == 7 else None))
    p.emit("dve", lambda e: e.tensor_copy(fbf[:, :], ps[:, 3080:3088]),
           wait=["fbT"], inc="dve", tag="fbE")
    p.emit("act", lambda e: e.activation(csrow, csrow, AF.Silu),
           wait=["ld_c"], inc="act", tag="silu")
    for j in range(8):
        p.emit("pe", lambda e, j=j: e.matmul(
            ps[:, 3072 + j:3072 + j + 1], csrow[0:1, 128 * j:128 * (j + 1)],
            eyeb[0:1, 0:1], start=True, stop=True),
            wait=(["silu", "ld_eyb", "fbE"] if j == 0 else []),
            inc=("pe" if j == 7 else None), tag=("csT" if j == 7 else None))
    p.emit("dve", lambda e: e.tensor_copy(csfm[:, :], ps[:, 3072:3080]),
           wait=["csT"], inc="dve", tag="csE")

    for pm in range(3):
        for ec in range(8):
            k = 8 * pm + ec
            w = [f"modm{(k-2)//8}_{(k-2)%8}"] if k >= 2 else ["cD_m15"]
            p.emit("pool", lambda e, pm=pm, ec=ec, s=ec % 2: e.dma_start(
                wmbuf[:, 2048 * s:2048 * (s + 1)],
                wm_h[128 * ec:128 * (ec + 1), 2048 * pm:2048 * (pm + 1)]),
                wait=w, inc=f"dwm{k % 2}", tag=f"wmld{k}")
            for n in range(4):
                w = [f"wmld{k}", "csE"]
                if pm > 0 and ec == 0 and n == 0:
                    w.append(f"modrowE{2*pm-1}")
                if pm == 0 and ec == 0 and n == 0:
                    w.append("cD_e15")
                p.emit("pe", lambda e, s=ec % 2, n=n, ec=ec: e.matmul(
                    ps[0:1, 512 * n:512 * (n + 1)], csfm[:, ec:ec + 1],
                    wmbuf[:, 2048 * s + 512 * n:2048 * s + 512 * (n + 1)],
                    start=(ec == 0), stop=(ec == 7)),
                    wait=w, inc=("pe" if n == 3 else None),
                    tag=(f"modm{pm}_{ec}" if n == 3 else None))
        if pm == 0:
            p.emit("pool", lambda e: e.dma_start(
                stA3[:, :, 0:64], wq_h.ap().rearrange("(c p) d -> p c d", p=128)),
                wait=["cD_m15"], inc="dqk", tag="ld_wq")
            p.emit("pool", lambda e: e.dma_start(
                stA3[:, :, 64:128], wv_h.ap().rearrange("(c p) d -> p c d", p=128)),
                inc="dqk", tag="ld_wv")
            p.emit("pool", lambda e: e.dma_start(
                stB3[:, :, :], wk_h.ap().rearrange("(c p) d -> p c d", p=128)),
                inc="dqk", tag="ld_wk")
        for hh2 in range(2):
            sp2 = 2 * pm + hh2
            p.emit("dve", lambda e, hh2=hh2: e.tensor_copy(
                modrow, ps[0:1, 1024 * hh2:1024 * (hh2 + 1)]),
                wait=[f"modm{pm}_7"], inc="dve", tag=f"modrowE{sp2}")
            for blk in range(8):
                w = []
                if blk == 0:
                    w = [f"modrowE{sp2}"]
                    if sp2 > 0:
                        w.append(f"mexE{sp2-1}")
                p.emit("pe", lambda e, blk=blk: e.transpose(
                    ps[:, 3072 + blk:3072 + blk + 1],
                    modrow[0:1, 128 * blk:128 * (blk + 1)], eyef[0:1, 0:1]),
                    wait=w, inc=("pe" if blk == 7 else None),
                    tag=(f"mexT{sp2}" if blk == 7 else None))
            p.emit("dve", lambda e, sp2=sp2: e.tensor_copy(
                modf[:, 8 * sp2:8 * (sp2 + 1)], ps[:, 3072:3080]),
                wait=[f"mexT{sp2}"], inc="dve", tag=f"mexE{sp2}")

    def derive(oi, oo, tagid, wtag):
        sa_i, ca_i, ga_i = oi, oi + 8, oi + 16
        pa, sp_, gp = oo, oo + 8, oo + 16
        p.emit("dve", lambda e: e.tensor_scalar(
            mvec[:, pa:pa + 8], modf[:, ca_i:ca_i + 8], 1.0, None, op0=ALU.add),
            wait=[wtag])
        p.emit("dve", lambda e: e.tensor_single_scalar(
            gmask[:, 0:8], modf[:, ga_i:ga_i + 8], 0.0, op=ALU.is_ge))
        p.emit("dve", lambda e: e.tensor_single_scalar(
            colsb[:, 0:8], modf[:, ga_i:ga_i + 8], 0.25, op=ALU.max))
        p.emit("dve", lambda e: e.tensor_single_scalar(
            colsb[:, 8:16], modf[:, ga_i:ga_i + 8], -0.25, op=ALU.min))
        p.emit("dve", lambda e: e.reciprocal(colsb[:, 8:16] if False else
                                             gmask[:, 8:16] if False else
                                             rn2[:, 0:8], mvec[:, pa:pa + 8]))
        p.emit("dve", lambda e: e.select(
            mvec[:, gp:gp + 8], gmask[:, 0:8], colsb[:, 0:8], colsb[:, 8:16]))
        p.emit("dve", lambda e: e.drain())
        p.emit("dve", lambda e: e.tensor_tensor(
            mvec[:, sp_:sp_ + 8], modf[:, sa_i:sa_i + 8], rn2[:, 0:8],
            op=ALU.mult),
            inc="dve", tag=tagid)

    derive(0, 0, "mva", "mexE2")
    derive(24, 24, "mvb", "mexE5")

    for ec in range(8):
        p.emit("dve", lambda e, ec=ec: e.tensor_scalar(
            stA3[:, ec, :], stA3[:, ec, :], mvec[:, ec:ec + 1], None, op0=ALU.mult),
            wait=(["mva", "ld_wq", "ld_wv", "ld_wk"] if ec == 0 else []))
        p.emit("dve", lambda e, ec=ec: e.tensor_scalar(
            stB3[:, ec, :], stB3[:, ec, :], mvec[:, ec:ec + 1], None, op0=ALU.mult),
            inc=("dve" if ec == 7 else None), tag=("stready" if ec == 7 else None))
    for ec in range(8):
        p.emit("dve", lambda e, ec=ec: e.tensor_scalar(
            w3v[:, ec, :], w3v[:, ec, :], mvec[:, 24 + ec:24 + ec + 1], None,
            op0=ALU.mult),
            wait=(["mvb", "cD_e15"] if ec == 0 else []),
            inc=("dve" if ec == 7 else None), tag=("wcready" if ec == 7 else None))
    if debug:
        p.emit("pool", lambda e: e.dma_start(dbg["mvec"][:, 0:48], mvec[:, :]),
               wait=["mva", "mvb"], inc="ds")
        p.emit("pool", lambda e: e.dma_start(dbg["mvec"][:, 48:96], modf[:, :]),
               inc="ds")
        p.emit("pool", lambda e: e.dma_start(dbg["wc"][:, :], w3s[:, :]),
               wait=["wcready"], inc="ds")
    for ec in range(8):
        p.emit("dve", lambda e, ec=ec: e.memset(extq[:, 2 * ec:2 * ec + 1], 1.0),
               wait=(["mva", "mvb"] if ec == 0 else []))
        p.emit("dve", lambda e, ec=ec: e.memset(extf[:, 2 * ec:2 * ec + 1], 1.0))
        p.emit("dve", lambda e, ec=ec: e.tensor_copy(
            extq[:, 2 * ec + 1:2 * ec + 2], mvec[:, 8 + ec:8 + ec + 1]))
        p.emit("dve", lambda e, ec=ec: e.tensor_copy(
            extf[:, 2 * ec + 1:2 * ec + 2], mvec[:, 32 + ec:32 + ec + 1]),
            inc=("dve" if ec == 7 else None), tag=("extready" if ec == 7 else None))

    prevtag = None
    prevfix = None
    prevwA = None
    prevwB = None
    for q in range(4):
        for r in range(2):
            for u in range(8):
                tgl = 2 * (8 * q + u) + r
                w = []
                if u == 0:
                    w = ["ln1done"]
                    if q == 0 and r == 0:
                        w += ["mexE5", "fbE", "csE"]
                    if r == 1:
                        w.append(f"q_repE{q}_0")
                    elif q > 0:
                        w.append(f"q_repE{q-1}_1")
                p.emit("pe", lambda e, u=u, tgl=tgl: e.transpose(
                    ps[0:1, 2560 + 128 * u:2560 + 128 * (u + 1)],
                    rntok[:, tgl:tgl + 1], eyef[:, :]),
                    wait=w, inc=("pe" if u == 7 else None),
                    tag=(f"q_rT{q}_{r}" if u == 7 else None))
            p.emit("dve", lambda e: e.tensor_copy(rowbf[0:1, 0:1024],
                                                  ps[0:1, 2560:3584]),
                   wait=[f"q_rT{q}_{r}"], inc="dve", tag=f"q_rE{q}_{r}")
            for n in range(2):
                p.emit("pe", lambda e, n=n: e.matmul(
                    ps[:, 2560 + 512 * n:2560 + 512 * (n + 1)], oner[:, :],
                    rowbf[0:1, 512 * n:512 * (n + 1)], start=True, stop=True),
                    wait=([f"q_rE{q}_{r}", "consts"] if n == 0 else []),
                    inc=("pe" if n == 1 else None),
                    tag=(f"q_bT{q}_{r}" if n == 1 else None))
            p.emit("dve", lambda e, r=r: e.tensor_copy(
                reps[:, 1024 * r:1024 * (r + 1)], ps[:, 2560:3584]),
                wait=[f"q_bT{q}_{r}"], inc="dve", tag=f"q_repE{q}_{r}")

        for hh in range(2):
            cg = 1024 * q + 512 * hh
            for ec in range(8):
                w = []
                if ec == 0:
                    w = [f"xE{NT-1}", "stready"]
                    if prevtag is not None:
                        w.append(prevtag)
                p.emit("dve", lambda e, ec=ec, cg=cg: e.tensor_copy(
                    qb3[:, ec, :], xfm3[:, ec, cg:cg + 512]),
                    wait=w, inc=("dve" if ec == 7 else None),
                    tag=(f"q_c{q}_{hh}" if ec == 7 else None))
            for ec in range(8):
                w = [f"q_c{q}_{hh}"]
                if ec == 0 and prevfix is not None:
                    w.append(prevfix)
                p.emit("pe", lambda e, ec=ec, hh=hh: e.matmul(
                    bank(hh), stA3[:, ec, :], qb3[:, ec, :],
                    start=(ec == 0), stop=(ec == 7)),
                    wait=w, inc=("pe" if ec == 7 else None),
                    tag=(f"q_A{q}_{hh}" if ec == 7 else None))
            for ec in range(8):
                p.emit("pe", lambda e, ec=ec, hh=hh: e.matmul(
                    bank(3 + hh, 0, 64), stB3[:, ec, :], qb3[:, ec, :],
                    start=(ec == 0), stop=(ec == 7)),
                    inc=("pe" if ec == 7 else None),
                    tag=(f"q_B{q}_{hh}" if ec == 7 else None))
            if q == 0 and hh == 0:
                for ec in range(8):
                    p.emit("pe", lambda e, ec=ec: e.matmul(
                        ps[:, 1024:1026], stA3[:, ec, :], extq[:, 2 * ec:2 * ec + 2],
                        start=(ec == 0), stop=(ec == 7)),
                        wait=(["extready"] if ec == 0 else []),
                        inc=("pe" if ec == 7 else None),
                        tag=("q_exA" if ec == 7 else None))
                for ec in range(8):
                    p.emit("pe", lambda e, ec=ec: e.matmul(
                        ps[0:64, 1026:1028], stB3[:, ec, :],
                        extq[:, 2 * ec:2 * ec + 2],
                        start=(ec == 0), stop=(ec == 7)),
                        inc=("pe" if ec == 7 else None),
                        tag=("q_exB" if ec == 7 else None))
                p.emit("dve", lambda e: e.tensor_copy(uqv[:, :], ps[:, 1024:1026]),
                       wait=["q_exA"])
                p.emit("dve", lambda e: e.tensor_copy(ukk[:, :], ps[0:64, 1026:1028]),
                       wait=["q_exB"], inc="dve", tag="q_ext")

            cs0 = 512 * hh
            b7 = ps[:, 3584:4096]
            b2f = ps[:, 1024:1536]
            wA = [f"q_A{q}_{hh}", f"q_repE{q}_1", "q_ext"]
            p.emit("dve", lambda e, hh=hh, cs0=cs0, b7=b7: e.scalar_tensor_tensor(
                b7, reps[:, 1024 + cs0:1024 + cs0 + 512], uqv[:, 0:1],
                bank(hh), op0=ALU.mult, op1=ALU.add),
                wait=wA)
            wA2 = [prevwB] if prevwB is not None else []
            p.emit("dve", lambda e, cs0=cs0, b7=b7, b2f=b2f: e.tensor_tensor(
                b2f, b7, reps[:, cs0:cs0 + 512], op=ALU.mult),
                wait=wA2, inc="dve", tag=f"q_fA{q}_{hh}")
            p.emit("dve", lambda e, cg=cg, b2f=b2f: e.tensor_scalar(
                qv[:, cg:cg + 512], b2f, uqv[:, 1:2], None, op0=ALU.add),
                wait=[f"q_fA{q}_{hh}"], inc="dve", tag=f"q_wA{q}_{hh}")
            p.emit("dve", lambda e, hh=hh, cs0=cs0, b7=b7: e.scalar_tensor_tensor(
                b7[0:64, :], reps[0:64, 1024 + cs0:1024 + cs0 + 512], ukk[:, 0:1],
                bank(3 + hh, 0, 64), op0=ALU.mult, op1=ALU.add),
                wait=[f"q_B{q}_{hh}"], inc="dve", tag=f"q_sB{q}_{hh}")
            p.emit("dve", lambda e, cs0=cs0, b7=b7, b2f=b2f: e.tensor_tensor(
                b2f[0:64, :], b7[0:64, :], reps[0:64, cs0:cs0 + 512], op=ALU.mult),
                wait=[f"q_wA{q}_{hh}"], inc="dve", tag=f"q_fB{q}_{hh}")
            p.emit("dve", lambda e, cg=cg, b2f=b2f: e.tensor_scalar(
                kk[0:64, cg:cg + 512], b2f[0:64, :], ukk[:, 1:2], None, op0=ALU.add),
                wait=[f"q_fB{q}_{hh}"], inc="dve", tag=f"q_wB{q}_{hh}")
            prevtag = f"q_B{q}_{hh}"
            prevfix = f"q_sB{q}_{hh}"
            prevwA = f"q_wA{q}_{hh}"
            prevwB = f"q_wB{q}_{hh}"

    if debug:
        p.emit("pool", lambda e: e.dma_start(dbg["qv"][:, :], qv[:, :]),
               wait=["q_wA3_1", "q_wB3_1"], inc="ds")
        p.emit("pool", lambda e: e.dma_start(dbg["kk"][:, :], kk[:, :]), inc="ds")

    for g in range(8):
        for u in range(4):
            c = 4 * g + u
            w = []
            if g == 0 and u == 0:
                w = ["q_wA3_1", "q_wB3_1", "ld_eyb", "q_repE3_1"]
            if u == 0 and g >= 2:
                w.append(f"vE{g-2}")
            bkc = 2560 + 512 * (g % 2)
            p.emit("pe", lambda e, c=c, u=u, bkc=bkc: e.matmul(
                ps[:, bkc + 64 * u:bkc + 64 * (u + 1)],
                qv[64:128, 128 * c:128 * (c + 1)], eyeb[64:128, 64:128],
                start=True, stop=True),
                wait=w, inc=("pe" if u == 3 else None),
                tag=(f"vT{g}" if u == 3 else None))
        p.emit("dve", lambda e, g=g, bkc=2560 + 512 * (g % 2): e.tensor_copy(
            vtok[:, 256 * g:256 * (g + 1)], ps[:, bkc:bkc + 256]),
            wait=[f"vT{g}"], inc="dve", tag=f"vE{g}")

    rowq = w2s[0:1, 0:T]
    qm2 = w2s[0:1, T:T + 1]
    kcols = rn2[:, 0:32]
    qm2c = rn2[:, 32:33]
    negall = rntok[:, 0:32]
    p.emit("dve", lambda e: e.tensor_tensor(
        a3[0:64, 1, :], kk[0:64, :], kk[0:64, :], op=ALU.mult),
        wait=["q_wB3_1", "cD_m15", "modm2_7"])
    p.emit("dve", lambda e: e.tensor_tensor(
        a3[0:64, 0, :], qv[0:64, :], qv[0:64, :], op=ALU.mult),
        wait=["q_wA3_1"], inc="dve", tag="nsq")
    for n in range(8):
        p.emit("pe", lambda e, n=n: e.matmul(
            ps[0:1, 512 * n:512 * (n + 1)], onec[0:64, 0:1],
            a3[0:64, 1, 512 * n:512 * (n + 1)], start=True, stop=True),
            wait=(["nsq", "vE7", "q_fB3_1", "q_ext"] if n == 0 else []),
            inc=("pe" if n == 7 else None), tag=("knT" if n == 7 else None))
    p.emit("dve", lambda e: e.tensor_copy(rowq, ps[0:1, 0:T]),
           wait=["knT"], inc="dve", tag="knE")
    for blk in range(32):
        w = ["knE"] if blk == 0 else []
        col = 2048 + 8 * (blk // 8) + (blk % 8)
        p.emit("pe", lambda e, blk=blk, col=col: e.matmul(
            ps[:, col:col + 1],
            rowq[0:1, 128 * blk:128 * (blk + 1)], eyeb[0:1, 0:1],
            start=True, stop=True),
            wait=w, inc=("pe" if blk == 31 else None),
            tag=("kcT" if blk == 31 else None))
    p.emit("dve", lambda e: e.tensor_copy(kcols, ps[:, 2048:2080]),
        wait=["kcT"], inc="dve", tag="kcE")
    for n in range(8):
        p.emit("pe", lambda e, n=n: e.matmul(
            ps[0:1, 512 * n:512 * (n + 1)], onec[0:64, 0:1],
            a3[0:64, 0, 512 * n:512 * (n + 1)], start=True, stop=True),
            wait=(["knE"] if n == 0 else []),
            inc=("pe" if n == 7 else None), tag=("qnT" if n == 7 else None))
    p.emit("dve", lambda e: e.tensor_copy(rowq, ps[0:1, 0:T]),
           wait=["qnT", "kcE"], inc="dve", tag="qnE")
    p.emit("dve", lambda e: e.reduce_max(qm2, rowq, axis=AX.X))
    p.emit("dve", lambda e: e.drain())
    p.emit("dve", lambda e: e.tensor_scalar(
        dmy[:, 0:1], dmy[:, 1:2], 1.0, None, op0=ALU.mult), inc="dve",
        tag="qmx")
    p.emit("pe", lambda e: e.matmul(ps[:, 2080:2081], oner[:, :], qm2,
                                    start=True, stop=True),
           wait=["qmx"], inc="pe", tag="qbT")
    p.emit("dve", lambda e: e.tensor_copy(qm2c, ps[:, 2080:2081]),
           wait=["qbT"], inc="dve", tag="qbE")
    p.emit("dve", lambda e: e.tensor_scalar(
        negall, kcols, qm2c, None, op0=ALU.mult))
    p.emit("dve", lambda e: e.tensor_scalar(
        dmy[:, 0:1], dmy[:, 1:2], 1.0, None, op0=ALU.mult))
    p.emit("dve", lambda e: e.tensor_scalar(
        dmy[:, 0:1], dmy[:, 1:2], 1.0, None, op0=ALU.mult), inc="dve",
        tag="mupre")
    p.emit("act", lambda e: e.activation(rn2[:, 0:32], negall, AF.Sqrt),
           wait=["mupre"], inc="act", tag="musq")
    p.emit("dve", lambda e: e.tensor_scalar(
        negall, rn2[:, 0:32], -1.01, None, op0=ALU.mult),
        wait=["musq"], inc="dve", tag="muall")
    p.emit("dve", lambda e: e.memset(kk[64:128, :], 0.0),
           wait=["cD_m15"], inc="dve", tag="kkz")

    p.wait("pe", ["vE7", "q_fB3_1", "q_ext", "muall", "qbE", "kkz"])
    def emit_av(c):
        cs = c % 2
        for nn in range(4):
            for cg2 in range(2):
                w = [f"vs{c}"]
                p.emit("pe", lambda e, c=c, cs=cs, nn=nn, cg2=cg2: e.matmul(
                    ps[64 * cg2:64 * (cg2 + 1), 512 * nn:512 * (nn + 1)],
                    vscb[:, 64 * cs:64 * (cs + 1)],
                    a3[:, cs, 2048 * cg2 + 512 * nn:2048 * cg2 + 512 * (nn + 1)],
                    start=(c == 0), stop=(c == NKC - 1),
                    tile_position=(0, 64 * cg2)),
                    wait=w, inc=("pe" if (nn == 3 and cg2 == 1) else None),
                    tag=(f"AV{c}" if (nn == 3 and cg2 == 1) else None))

    for c in range(NKC):
        cs = c % 2
        for s in range(4):
            gc = 2048 + 1024 * (s % 2)
            if s == 2 and c >= 1:
                emit_av(c - 1)
            for nn in range(2):
                i0 = 1024 * s + 512 * nn
                if c == 0 and s < 2:
                    w = []
                elif s >= 2:
                    w = [f"exp{c}_{s-2}"]
                else:
                    w = [f"exp{c-1}_{s+2}"]
                p.emit("pe", lambda e, c=c, i0=i0, gc=gc, nn=nn: e.matmul(
                    ps[:, gc + 512 * nn:gc + 512 * (nn + 1)],
                    kk[:, 128 * c:128 * (c + 1)], qv[:, i0:i0 + 512],
                    start=True, stop=True),
                    wait=w, inc=("pe" if nn == 1 else None),
                    tag=(f"S{c}_{s}" if nn == 1 else None))
            w = [f"S{c}_{s}", "muall"]
            if c >= 2:
                w += [f"AV{c-2}", f"vs{c-2}"]
            p.emit("act", lambda e, c=c, s=s, cs=cs, gc=gc: e.activation(
                a3[:, cs, 1024 * s:1024 * (s + 1)], ps[:, gc:gc + 1024], AF.Exp,
                bias=negall[:, c:c + 1],
                accum_out=zpart[:, 4 * cs + s:4 * cs + s + 1]),
                wait=w, inc="act", tag=f"exp{c}_{s}")
        p.emit("dve", lambda e, cs=cs: e.tensor_tensor(
            zval[:, cs:cs + 1], zpart[:, 4 * cs:4 * cs + 1],
            zpart[:, 4 * cs + 1:4 * cs + 2], op=ALU.add),
            wait=[f"exp{c}_{s2}" for s2 in range(4)])
        p.emit("dve", lambda e, cs=cs: e.tensor_tensor(
            zrec[:, cs:cs + 1], zpart[:, 4 * cs + 2:4 * cs + 3],
            zpart[:, 4 * cs + 3:4 * cs + 4], op=ALU.add))
        p.emit("dve", lambda e: e.tensor_scalar(
            dmy[:, 0:1], dmy[:, 1:2], 1.0, None, op0=ALU.mult))
        p.emit("dve", lambda e, cs=cs: e.tensor_tensor(
            zval[:, cs:cs + 1], zval[:, cs:cs + 1], zrec[:, cs:cs + 1],
            op=ALU.add))
        p.emit("dve", lambda e: e.drain())
        p.emit("dve", lambda e, cs=cs: e.reciprocal(zrec[:, cs:cs + 1],
                                                    zval[:, cs:cs + 1]))
        p.emit("dve", lambda e: e.drain())
        w = [f"vE{c//4}"]
        if c >= 2:
            w.append(f"AV{c-2}")
        p.emit("dve", lambda e, c=c, cs=cs: e.tensor_scalar(
            vscb[:, 64 * cs:64 * (cs + 1)], vtok[:, 64 * c:64 * (c + 1)],
            zrec[:, cs:cs + 1], None, op0=ALU.mult),
            wait=w, inc="dve", tag=f"vs{c}")
        if debug:
            p.emit("pool", lambda e, c=c, cs=cs: e.dma_start(
                dbg["z"][:, 2 * c:2 * c + 1], zval[:, cs:cs + 1],
                allow_slow_non_contiguous=True),
                wait=[f"vs{c}"], inc="ds")
            p.emit("pool", lambda e, c=c: e.dma_start(
                dbg["z"][:, 2 * c + 1:2 * c + 2], negall[:, c:c + 1],
                allow_slow_non_contiguous=True), inc="ds")
    emit_av(NKC - 1)

    p.emit("dve", lambda e: e.tensor_copy(avs[:, :], ps[:, 0:2048]),
           wait=[f"AV{NKC-1}"], inc="dve", tag="avE")
    for i, (dp, dc0, sp0) in enumerate([(0, 0, 0), (0, 2048, 64),
                                        (64, 0, 0), (64, 2048, 64)]):
        p.emit("pool", lambda e, dp=dp, dc0=dc0, sp0=sp0: e.dma_start(
            avd[dp:dp + 64, dc0:dc0 + 2048], avs[sp0:sp0 + 64, 0:2048]),
            wait=(["avE"] if i == 0 else []), inc="ds",
            tag=("avdup" if i == 3 else None))
    if debug:
        p.emit("pool", lambda e: e.dma_start(dbg["av"][:, :], avd[:, :]),
               wait=["avdup"], inc="ds")

    for q in range(4):
        for ec in range(8):
            slot = ec % 2
            if q == 0:
                p.emit("dve", lambda e, j=ec: e.scalar_tensor_tensor(
                    xfm3[:, j, :], avd[:, :], mvec[:, 16 + j:16 + j + 1],
                    xfm3[:, j, :], op0=ALU.mult, op1=ALU.add),
                    wait=(["avdup", "mva"] if ec == 0 else []),
                    inc="dve", tag=f"res1_{ec}")
                if debug:
                    p.emit("pool", lambda e, j=ec: e.dma_start(
                        dbg["x1"][:, T * j:T * (j + 1)], xfm3[:, j, :]),
                        wait=[f"res1_{ec}"], inc="ds")
            w = [f"res1_{ec}"]
            if q > 0 or ec >= 2:
                w.append(f"lnm{q}_{ec-2}" if ec >= 2 else f"lnm{q-1}_{6+ec}")
            p.emit("dve", lambda e, ec=ec, q=q, s=slot: e.tensor_copy(
                plbuf[:, 1024 * s:1024 * (s + 1)],
                xfm3[:, ec, 1024 * q:1024 * (q + 1)]),
                wait=w)
            p.emit("dve", lambda e, s=slot: e.tensor_tensor(
                sqb[:, 1024 * s:1024 * (s + 1)], plbuf[:, 1024 * s:1024 * (s + 1)],
                plbuf[:, 1024 * s:1024 * (s + 1)], op=ALU.mult),
                inc="dve", tag=f"lnc{q}_{ec}")
            for n in range(2):
                w = [f"lnc{q}_{ec}", "consts"]
                if q == 0 and ec == 0 and n == 0:
                    w.append("avE")
                if q > 0 and ec == 0 and n == 0:
                    w.append(f"lnrowE{q-1}")
                p.emit("pe", lambda e, s=slot, n=n, ec=ec: e.matmul(
                    ps[0:1, 512 * n:512 * (n + 1)], onec[:, 0:1],
                    plbuf[:, 1024 * s + 512 * n:1024 * s + 512 * (n + 1)],
                    start=(ec == 0), stop=(ec == 7)),
                    wait=w)
                p.emit("pe", lambda e, s=slot, n=n, ec=ec: e.matmul(
                    ps[0:1, 1024 + 512 * n:1024 + 512 * (n + 1)], onec[:, 0:1],
                    sqb[:, 1024 * s + 512 * n:1024 * s + 512 * (n + 1)],
                    start=(ec == 0), stop=(ec == 7)),
                    inc=("pe" if n == 1 else None),
                    tag=(f"lnm{q}_{ec}" if n == 1 else None))
        p.emit("dve", lambda e: e.tensor_copy(ln2rows, ps[0:1, 0:2048]),
               wait=[f"lnm{q}_7"], inc="dve", tag=f"lnrowE{q}")
        for u in range(16):
            w = []
            if u == 0:
                w = [f"lnrowE{q}"]
                if q > 0:
                    w.append(f"lncolE{q-1}")
            p.emit("pe", lambda e, u=u: e.matmul(
                ps[:, 2048 + u:2048 + u + 1], ln2rows[0:1, 128 * u:128 * (u + 1)],
                eyeb[0:1, 0:1], start=True, stop=True),
                wait=w, inc=("pe" if u == 15 else None),
                tag=(f"lncT{q}" if u == 15 else None))
        p.emit("dve", lambda e: e.tensor_copy(colsb[:, :], ps[:, 2048:2064]),
               wait=[f"lncT{q}"], inc="dve", tag=f"lncolE{q}")
        mq = colsb[:, 0:8]
        sq_ = colsb[:, 8:16]
        p.emit("dve", lambda e, mq=mq: e.tensor_scalar(mq, mq, 1.0 / E, None,
                                                       op0=ALU.mult))
        p.emit("dve", lambda e, sq_=sq_: e.tensor_scalar(sq_, sq_, 1.0 / E, None,
                                                         op0=ALU.mult))
        p.emit("dve", lambda e, q=q, mq=mq: e.tensor_copy(
            rn2[:, 16 * q:16 * q + 16:2], mq))
        p.emit("dve", lambda e, mq=mq: e.tensor_tensor(mq, mq, mq, op=ALU.mult))
        p.emit("dve", lambda e: e.tensor_scalar(
            dmy[:, 0:1], dmy[:, 1:2], 1.0, None, op0=ALU.mult))
        p.emit("dve", lambda e, mq=mq, sq_=sq_: e.tensor_tensor(
            sq_, sq_, mq, op=ALU.subtract))
        p.emit("dve", lambda e: e.tensor_scalar(
            dmy[:, 0:1], dmy[:, 1:2], 1.0, None, op0=ALU.mult))
        p.emit("dve", lambda e, sq_=sq_: e.tensor_scalar(sq_, sq_, EPS, None,
                                                         op0=ALU.add))
        p.emit("dve", lambda e: e.drain())
        p.emit("dve", lambda e, q=q, sq_=sq_: e.reciprocal(
            rn2[:, 16 * q + 1:16 * q + 16:2], sq_))
        p.emit("dve", lambda e: e.drain())
        p.emit("dve", lambda e: e.tensor_scalar(
            dmy[:, 0:1], dmy[:, 1:2], 1.0, None, op0=ALU.mult),
            inc="dve", tag=f"lnv{q}")
    p.emit("act", lambda e: e.activation(rn2[:, 1:64:2], rn2[:, 1:64:2], AF.Sqrt),
           wait=[f"lnv{qq}" for qq in range(4)], inc="act", tag="ln2r")
    p.emit("dve", lambda e: e.tensor_tensor(
        hstat[:, 64:96], rn2[:, 0:64:2], rn2[:, 1:64:2], op=ALU.mult),
        wait=["ln2r"])
    p.emit("dve", lambda e: e.tensor_scalar(
        dmy[:, 0:1], dmy[:, 1:2], 1.0, None, op0=ALU.mult))
    p.emit("dve", lambda e: e.tensor_scalar(
        dmy[:, 0:1], dmy[:, 1:2], 1.0, None, op0=ALU.mult))
    p.emit("dve", lambda e: e.tensor_scalar(
        rn2[:, 0:64:2], hstat[:, 64:96], -1.0, None, op0=ALU.mult),
        inc="dve", tag="ln2nm")

    for dc in range(8):
        for ec in range(8):
            w = []
            if dc == 0 and ec == 0:
                w = ["wcready", "extready", "lnm3_7", "lnrowE3"]
            p.emit("pe", lambda e, dc=dc, ec=ec: e.matmul(
                ps[:, 2 * dc:2 * dc + 2], w3v[:, ec, 128 * dc:128 * (dc + 1)],
                extf[:, 2 * ec:2 * ec + 2], start=(ec == 0), stop=(ec == 7)),
                wait=w, inc=("pe" if (ec == 7 and dc == 7) else None),
                tag=("fext" if (ec == 7 and dc == 7) else None))
    p.emit("dve", lambda e: e.tensor_copy(u3b[:, :], ps[:, 0:16]),
           wait=["fext"], inc="dve", tag="fextE")
    p.emit("dve", lambda e: e.tensor_scalar(
        dmy[:, 0:1], dmy[:, 1:2], 1.0, None, op0=ALU.mult))
    p.emit("dve", lambda e: e.tensor_scalar(
        dmy[:, 0:1], dmy[:, 1:2], 1.0, None, op0=ALU.mult))
    p.emit("dve", lambda e: e.tensor_tensor(bias3[:, :], u3b[:, 1:16:2], fbf[:, :],
                                            op=ALU.add),
           wait=["fextE", "fbE"], inc="dve", tag="bias3r")
    if debug:
        p.emit("pool", lambda e: e.dma_start(dbg["u3"][:, 0:16], u3b[:, :]),
               wait=["fextE"], inc="ds")
        p.emit("pool", lambda e: e.dma_start(dbg["u3"][:, 16:24], bias3[:, :]),
               wait=["bias3r"], inc="ds")

    for q in range(4):
        for r in range(2):
            roff = 2048 + 1024 * r
            for u in range(8):
                tq = 16 * q + 2 * u + (1 - r)
                w = []
                if u == 0:
                    w = ["ln2nm"]
                    if r == 0:
                        w.append("fextE" if q == 0 else f"tanh{q-1}_7")
                    if r == 0 and q > 0:
                        w += [f"oE{8*q-2}", f"oEa{8*q-2}"]
                p.emit("pe", lambda e, u=u, tq=tq, roff=roff: e.transpose(
                    ps[0:1, roff + 128 * u:roff + 128 * (u + 1)],
                    rn2[:, tq:tq + 1], eyef[:, :]),
                    wait=w, inc=("pe" if u == 7 else None),
                    tag=(f"f_rT{q}_{r}" if u == 7 else None))
            p.emit("dve", lambda e, r=r, roff=roff: e.tensor_copy(
                rowbf[0:1, 1024 * r:1024 * (r + 1)], ps[0:1, roff:roff + 1024]),
                wait=[f"f_rT{q}_{r}"], inc="dve", tag=f"f_rE{q}_{r}")
        for r in range(2):
            roff = 2048 + 1024 * r
            for n in range(2):
                p.emit("pe", lambda e, r=r, n=n, roff=roff: e.matmul(
                    ps[:, roff + 512 * n:roff + 512 * (n + 1)], oner[:, :],
                    rowbf[0:1, 1024 * r + 512 * n:1024 * r + 512 * (n + 1)],
                    start=True, stop=True),
                    wait=([f"f_rE{q}_1"] if (r == 0 and n == 0) else []),
                    inc=("pe" if n == 1 else None),
                    tag=(f"f_bT{q}_{r}" if n == 1 else None))
            p.emit("dve", lambda e, r=r, roff=roff: e.tensor_copy(
                reps[:, 1024 * r:1024 * (r + 1)], ps[:, roff:roff + 1024]),
                wait=[f"f_bT{q}_{r}"], inc="dve", tag=f"f_repE{q}_{r}")
        for ec in range(8):
            wsc = []
            if ec == 0:
                wsc = [f"f_repE{q}_1"]
                if q > 0:
                    wsc.append(f"fP{q-1}_7")
            p.emit("dve", lambda e, ec=ec, q=q: e.tensor_tensor(
                xs3[:, ec, :], xfm3[:, ec, 1024 * q:1024 * (q + 1)],
                reps[:, 0:1024], op=ALU.mult),
                wait=wsc, inc="dve", tag=f"f_sc{q}_{ec}")
        for dc in range(8):
            pb = dc % 2
            pc0 = 1024 * pb
            t20 = 2048 + 1024 * pb
            for ec in range(8):
                for n in range(2):
                    w = []
                    if dc == 0 and n == 0:
                        w.append(f"f_sc{q}_{ec}")
                    if ec == 0 and n == 0:
                        if dc >= 1:
                            w.append(f"f_sc{q}_7")
                        if dc >= 2:
                            w.append(f"tanh{q}_{dc-2}")
                        elif q > 0:
                            w.append(f"tanh{q-1}_{6+dc}")
                            w.append(f"oE{8*q-3}")
                            w.append(f"oEa{8*q-3}")
                        if q == 0 and dc <= 1:
                            w.append("fextE")
                        if dc <= 1:
                            w.append(f"f_repE{q}_1")
                    p.emit("pe", lambda e, dc=dc, ec=ec, n=n, pc0=pc0: e.matmul(
                        ps[:, pc0 + 512 * n:pc0 + 512 * (n + 1)],
                        w3v[:, ec, 128 * dc:128 * (dc + 1)],
                        xs3[:, ec, 512 * n:512 * (n + 1)],
                        start=(ec == 0), stop=(ec == 7)),
                        wait=w, inc=("pe" if (ec == 7 and n == 1) else None),
                        tag=(f"fP{q}_{dc}" if (ec == 7 and n == 1) else None))
            p.emit("dve", lambda e, pc0=pc0, t20=t20, dc=dc: e.scalar_tensor_tensor(
                ps[:, t20:t20 + 1024], reps[:, 1024:2048], u3b[:, 2 * dc:2 * dc + 1],
                ps[:, pc0:pc0 + 1024], op0=ALU.mult, op1=ALU.add),
                wait=[f"fP{q}_{dc}", f"f_repE{q}_1", "bias3r"],
                inc="dve", tag=f"fT2{q}_{dc}")
            wth = [f"fT2{q}_{dc}"]
            if dc >= 2:
                wth.append(f"res2_{q}_{dc-2}")
            elif q > 0:
                wth.append(f"res2_{q-1}_{6+dc}")
            elif dc == 0:
                wth.append("avdup")
                wth.append(f"res1_{NCH-1}")
            p.emit("act", lambda e, dc=dc, t20=t20: e.activation(
                tnh[:, 1024 * (dc % 2):1024 * (dc % 2) + 1024], ps[:, t20:t20 + 1024],
                AF.Tanh, bias=bias3[:, dc:dc + 1]),
                wait=wth, inc="act", tag=f"tanh{q}_{dc}")
            p.emit("dve", lambda e, dc=dc, q=q: e.scalar_tensor_tensor(
                xfm3[:, dc, 1024 * q:1024 * (q + 1)],
                tnh[:, 1024 * (dc % 2):1024 * (dc % 2) + 1024],
                mvec[:, 40 + dc:40 + dc + 1], xfm3[:, dc, 1024 * q:1024 * (q + 1)],
                op0=ALU.mult, op1=ALU.add),
                wait=[f"tanh{q}_{dc}", "mvb"], inc="dve", tag=f"res2_{q}_{dc}")

        for ti in range(8):
            t = 8 * q + ti
            b0c = 1024 * (ti % 4)
            for j in range(NCH):
                w = []
                if j == 0:
                    if ti == 0:
                        w = [f"fT2{q}_7", f"tanh{q}_7"] + \
                            [f"res2_{q}_{k}" for k in range(NCH)]
                    if ti >= 4:
                        w = [f"oE{t-4}", f"oEa{t-4}"]
                col = b0c + 512 * (j // 4) + 128 * (j % 4)
                p.emit("pe", lambda e, t=t, j=j, col=col: e.transpose(
                    ps[:, col:col + 128], xfm3[:, j, 128 * t:128 * (t + 1)],
                    eyef[:, :]),
                    wait=w, inc=("pe" if j == NCH - 1 else None),
                    tag=(f"oT{t}" if j == NCH - 1 else None))
            p.emit("dve", lambda e, t=t, b0c=b0c: e.tensor_copy(
                xfm3[:, 0:4, 128 * t:128 * t + 128], ps[:, b0c:b0c + 512]),
                wait=[f"oT{t}"], inc="dve", tag=f"oE{t}")
            p.emit("act", lambda e, t=t, b0c=b0c: e.copy(
                xfm3[:, 4:8, 128 * t:128 * t + 128],
                ps[:, b0c + 512:b0c + 1024]),
                wait=[f"oT{t}"], inc="act", tag=f"oEa{t}")
            p.emit("sp", lambda e, t=t: e.dma_start(
                out_h[128 * t:128 * (t + 1), :].rearrange("p (c m) -> p c m",
                                                          m=128),
                xfm3[:, :, 128 * t:128 * (t + 1)]),
                wait=[f"oE{t}", f"oEa{t}"], inc="do", tag=f"ost{t}")

    p.wait("sp", [f"ost{t}" for t in range(NT)])

    p.lower(nc, semh)
    es.close()
    return nc


_NC_CACHE = {}


def _get_nc(debug=False):
    if debug not in _NC_CACHE:
        _NC_CACHE[debug] = build(debug)
    return _NC_CACHE[debug]


def make_in_maps(x, wq, wk, wv, w1, w2, w3, fb, c, wm):
    eye = np.eye(128, dtype=np.float32)
    eyb = np.eye(128).astype(ml_dtypes.bfloat16)
    in_maps = []
    for b in range(B):
        in_maps.append({
            "x": x[b], "wq": wq, "wk": wk, "wv": wv,
            "w1": w1, "w2": w2, "w3": w3, "fbias": fb,
            "crow": c[b:b + 1], "wmod": wm, "eye": eye, "eyeb": eyb,
        })
    return in_maps


def kernel(L0_attn_x, L0_attn_Wq, L0_attn_Wk, L0_attn_Wv, L0_ffn_weight,
           L0_ffn_aux_weight1, L0_ffn_aux_weight2, L0_ffn_bias, L0_c, L0_wmod):
    f = lambda a: np.ascontiguousarray(np.asarray(a, dtype=np.float32))
    x = f(L0_attn_x)
    in_maps = make_in_maps(x, f(L0_attn_Wq), f(L0_attn_Wk), f(L0_attn_Wv),
                           f(L0_ffn_weight), f(L0_ffn_aux_weight1),
                           f(L0_ffn_aux_weight2), f(L0_ffn_bias).reshape(1, E),
                           f(L0_c), f(L0_wmod))
    nc = _get_nc(False)
    res = run_bass_kernel_spmd(nc, in_maps, core_ids=list(range(B)))
    return np.stack([res.results[b]["out"] for b in range(B)], axis=0)


# version 13 (final)
# speedup vs baseline: 1.0397x; 1.0112x over previous
import numpy as np
import ml_dtypes
import concourse.bass as bass
import concourse.mybir as mybir
from contextlib import ExitStack
from concourse.bass_utils import run_bass_kernel_spmd

F32 = mybir.dt.float32
BF16 = mybir.dt.bfloat16
U8 = mybir.dt.uint8
ALU = mybir.AluOpType
AF = mybir.ActivationFunctionType
AX = mybir.AxisListType

B, T, E, D = 8, 4096, 1024, 64
NCH = E // 128
NT = T // 128
NKC = T // 128
EPS = 1e-5

ENGS = ("pe", "act", "dve", "pool", "sp")
DSEMS = tuple(f"dx{t}" for t in range(32)) + ("dwm0", "dwm1", "dfb", "deye", "deyb", "dc", "dw1", "dw2", "dw3", "dqk", "do", "ds")


class Prog:
    def __init__(self):
        self.stream = {e: [] for e in ENGS}
        self.cnt = {s: 0 for s in ENGS + DSEMS}
        self.mile = {}
        self.lastwait = {}

    def _waits(self, eng, wait):
        ws = []
        for w in wait:
            if w is None:
                continue
            sem, val, peng = self.mile[w] if isinstance(w, str) else w
            if peng == eng and sem in ENGS:
                continue
            key = (eng, sem)
            if self.lastwait.get(key, -1) >= val:
                continue
            self.lastwait[key] = val
            ws.append((sem, val))
        return ws

    def emit(self, eng, fn, wait=(), inc=None, tag=None):
        ws = self._waits(eng, wait)
        amt = 0
        if inc is not None:
            amt = 16 if inc in DSEMS else 1
            self.cnt[inc] += amt
            if tag is not None:
                self.mile[tag] = (inc, self.cnt[inc], eng)
        elif tag is not None:
            raise ValueError("tag requires inc")

        def closure(e, sems):
            for sem, val in ws:
                e.wait_ge(sems[sem], val)
            inst = fn(e)
            if inc is not None:
                inst.then_inc(sems[inc], amt)
        self.stream[eng].append(closure)

    def wait(self, eng, wait):
        ws = self._waits(eng, wait)
        if not ws:
            return

        def closure(e, sems):
            for sem, val in ws:
                e.wait_ge(sems[sem], val)
        self.stream[eng].append(closure)

    def lower(self, nc, sems):
        with nc.Block() as block:
            for eng, deco in (("pe", block.tensor), ("act", block.scalar),
                              ("dve", block.vector), ("pool", block.gpsimd),
                              ("sp", block.sync)):
                stream = self.stream[eng]

                def run(e, stream=stream):
                    for closure in stream:
                        closure(e, sems)
                deco(run)


def build(debug=False):
    nc = bass.Bass("TRN2", target_bir_lowering=False)

    x_hbm = nc.dram_tensor("x", [T, E], F32, kind="ExternalInput")
    wq_h = nc.dram_tensor("wq", [E, D], F32, kind="ExternalInput")
    wk_h = nc.dram_tensor("wk", [E, D], F32, kind="ExternalInput")
    wv_h = nc.dram_tensor("wv", [E, D], F32, kind="ExternalInput")
    w1_h = nc.dram_tensor("w1", [E, E], F32, kind="ExternalInput")
    w2_h = nc.dram_tensor("w2", [E, E], F32, kind="ExternalInput")
    w3_h = nc.dram_tensor("w3", [E, E], F32, kind="ExternalInput")
    fb_h = nc.dram_tensor("fbias", [1, E], F32, kind="ExternalInput")
    c_h = nc.dram_tensor("crow", [1, E], F32, kind="ExternalInput")
    wm_h = nc.dram_tensor("wmod", [E, 6 * E], F32, kind="ExternalInput")
    eye_h = nc.dram_tensor("eye", [128, 128], F32, kind="ExternalInput")
    eyb_h = nc.dram_tensor("eyeb", [128, 128], BF16, kind="ExternalInput")
    out_h = nc.dram_tensor("out", [T, E], F32, kind="ExternalOutput")
    dbg = {}
    if debug:
        dbg["mvec"] = nc.dram_tensor("d_mvec", [128, 96], F32, kind="ExternalOutput")
        dbg["qv"] = nc.dram_tensor("d_qv", [128, T], F32, kind="ExternalOutput")
        dbg["kk"] = nc.dram_tensor("d_kk", [128, T], F32, kind="ExternalOutput")
        dbg["stats"] = nc.dram_tensor("d_stats", [128, 128], F32, kind="ExternalOutput")
        dbg["z"] = nc.dram_tensor("d_z", [128, 64], F32, kind="ExternalOutput")
        dbg["av"] = nc.dram_tensor("d_av", [128, T], F32, kind="ExternalOutput")
        dbg["x1"] = nc.dram_tensor("d_x1", [128, NCH * T], F32, kind="ExternalOutput")
        dbg["wc"] = nc.dram_tensor("d_wc", [128, NCH * E], F32, kind="ExternalOutput")
        dbg["u3"] = nc.dram_tensor("d_u3", [128, 32], F32, kind="ExternalOutput")
        dbg["xf"] = nc.dram_tensor("d_xf", [128, NCH * T], F32, kind="ExternalOutput")
        dbg["hs"] = nc.dram_tensor("d_hs", [128, 128], F32, kind="ExternalOutput")
        dbg["ln1"] = nc.dram_tensor("d_ln1", [128, 64], F32, kind="ExternalOutput")

    es = ExitStack()
    sb = lambda n, sh, dt=F32: es.enter_context(nc.sbuf_tensor(n, sh, dt))

    xfm = sb("xfm", [128, NCH * T])
    fbrow = sb("fbrow", [1, E])
    hstat = sb("hstat", [128, 128])
    rntok = sb("rntok", [128, 64])
    rn2 = sb("rn2", [128, 64])
    modf = sb("modf", [128, 48])
    mvec = sb("mvec", [128, 48])
    colsb = sb("colsb", [128, 16])
    uqv = sb("uqv", [128, 2])
    ukk = sb("ukk", [64, 2])
    u3b = sb("u3b", [128, 16])
    bias3 = sb("bias3", [128, 8])
    fbf = sb("fbf", [128, 8])
    negmu = sb("negmu", [128, 2])
    zpart = sb("zpart", [128, 8])
    zval = sb("zval", [128, 2])
    zrec = sb("zrec", [128, 2])
    dmy = sb("dmy", [128, 2])
    eyef = sb("eyef", [128, 128])

    w1s = sb("w1s", [128, NCH * E], BF16)
    w2s = sb("w2s", [128, NCH * E], BF16)
    w3s = sb("w3s", [128, NCH * E], BF16)
    m1s = sb("m1s", [128, NCH * E], BF16)
    scrb = sb("scrb", [128, 2048], BF16)
    vscb = sb("vscb", [128, 2 * D], BF16)
    stA = sb("stA", [128, NCH * 128], BF16)
    stB = sb("stB", [128, NCH * 64], BF16)
    extq = sb("extq", [128, 16], BF16)
    extf = sb("extf", [128, 16], BF16)
    onec = sb("onec", [128, 2], BF16)
    oner = sb("oner", [1, 128], BF16)
    csfm = sb("csfm", [128, 8], BF16)
    eyeb = sb("eyebs", [128, 128], BF16)
    gmask = sb("gmask", [128, 16], U8)

    ps = es.enter_context(nc.psum_tensor("ps", [128, 4096], F32))

    semh = {}
    for s in ENGS + DSEMS:
        semh[s] = es.enter_context(nc.semaphore("sem_" + s))

    p = Prog()

    xfm3 = xfm[:, :].rearrange("p (c m) -> p c m", m=T)
    w1v = w1s[:, :].rearrange("p (c m) -> p c m", m=E)
    w2v = w2s[:, :].rearrange("p (c m) -> p c m", m=E)
    w3v = w3s[:, :].rearrange("p (c m) -> p c m", m=E)
    m1v = m1s[:, :].rearrange("p (c m) -> p c m", m=E)
    stA3 = stA[:, :].rearrange("p (c m) -> p c m", m=128)
    stB3 = stB[:, :].rearrange("p (c m) -> p c m", m=64)

    qv = w1s[:, 0:T]
    kk = w1s[:, T:2 * T]
    qb3 = w2s[:, 0:4096].rearrange("p (c m) -> p c m", m=512)
    avs = w2s[:, 0:2048]
    avd = w2s[:, 4096:8192]
    plbuf = w2s[:, 0:2048]
    sqb = w2s[:, 2048:4096]
    tnh = w2s[:, 4096:6144]
    a3 = m1s[:, :].rearrange("p (c m) -> p c m", m=T)
    xs3 = m1s[:, :].rearrange("p (c m) -> p c m", m=1024)
    csrow = scrb[0:1, 0:E]
    reps = scrb[:, 0:2048]
    vtok = scrb[:, 0:2048]
    wmbuf = m1s[:, 4096:8192]
    modrow = fbrow[0:1, 0:1024]
    rowbf = m1s[0:1, 0:2048]
    ln2rows = scrb[0:1, 0:2048]
    jitb = stA[:, 0:1024]
    bnb = colsb[:, 0:12]

    def bank(i, lo=0, hi=128):
        return ps[lo:hi, 512 * i:512 * (i + 1)]

    p.emit("sp", lambda e: e.dma_start(fbrow[:, :], fb_h[:, :]), inc="dfb", tag="ld_fb")
    p.emit("sp", lambda e: e.dma_start(eyef[:, :], eye_h[:, :]), inc="deye", tag="ld_eye")
    p.emit("sp", lambda e: e.dma_start(eyeb[:, :], eyb_h[:, :]), inc="deyb", tag="ld_eyb")

    p.emit("pool", lambda e: e.dma_start(csrow, c_h[:, :]), inc="dc", tag="ld_c")
    p.emit("pool", lambda e: e.dma_start(
        w2v[:, :, :], w2_h.ap().rearrange("(c p) d -> p c d", p=128)),
        inc="dw2", tag="ld_w2")
    p.emit("pool", lambda e: e.dma_start(
        w3v[:, :, :], w3_h.ap().rearrange("(c p) d -> p c d", p=128)),
        inc="dw3", tag="ld_w3")
    p.emit("pool", lambda e: e.dma_start(
        w1v[:, :, :], w1_h.ap().rearrange("(c p) d -> p c d", p=128)),
        inc="dw1", tag="ld_w1")

    for t in range(NT):
        p.emit("sp", lambda e, t=t: e.dma_start(
            xfm3[:, :, 128 * t:128 * (t + 1)],
            x_hbm[128 * t:128 * (t + 1), :].rearrange("p (c m) -> p c m", m=128)),
            inc=f"dx{t}", tag=f"xld{t}")

    p.emit("dve", lambda e: e.memset(onec[:, :], 1.0))
    p.emit("dve", lambda e: e.memset(dmy[:, :], 1.0))
    p.emit("dve", lambda e: e.memset(oner[:, :], 1.0), inc="dve", tag="consts")

    jslots = (jitb, scrb[:, 1024:2048])
    def mul_jit(src3, rhs3, dst3, first_wait, tg):
        nt = 0
        for oc in range(8):
            jb = jslots[oc % 2]
            for k in range(8):
                w = []
                if k == 0:
                    w = list(first_wait) if oc == 0 else []
                    if oc >= 1:
                        w.append(f"{tg}_je{oc-1}")
                col = 1024 + 512 * (k // 4) + 128 * (k % 4)
                p.emit("pe", lambda e, s=src3, oc=oc, k=k, col=col: e.matmul(
                    ps[:, col:col + 128], s[:, oc, 128 * k:128 * (k + 1)],
                    eyeb[:, :], start=True, stop=True),
                    wait=w, inc=("pe" if k == 7 else None),
                    tag=(f"{tg}_jt{oc}" if k == 7 else None))
            for g in range(2):
                w = [f"{tg}_jt{oc}"]
                if g == 0 and nt >= 4:
                    w.append(f"{tg}_m{nt-3}")
                p.emit("act", lambda e, g=g, jb=jb: e.copy(
                    jb[:, 512 * g:512 * (g + 1)],
                    ps[:, 512 * (2 + g):512 * (2 + g) + 512]),
                    wait=w, inc=("act" if g == 1 else None),
                    tag=(f"{tg}_je{oc}" if g == 1 else None))
            for n in range(2):
                bk = 4 + nt % 2
                for k in range(8):
                    w = []
                    if k == 0:
                        w.append(f"{tg}_je{oc}")
                        if nt >= 2:
                            w.append(f"{tg}_e{nt-2}")
                        if oc == 0 and n == 0:
                            w += list(first_wait)
                    p.emit("pe", lambda e, r=rhs3, oc=oc, n=n, k=k, bk=bk, jb=jb:
                           e.matmul(bank(bk), jb[:, 128 * k:128 * (k + 1)],
                                    r[:, k, 512 * n:512 * (n + 1)],
                                    start=(k == 0), stop=(k == 7)),
                           wait=w, inc=("pe" if k == 7 else None),
                           tag=(f"{tg}_m{nt}" if k == 7 else None))
                p.emit("act", lambda e, d=dst3, oc=oc, n=n, bk=bk: e.copy(
                    d[:, oc, 512 * n:512 * (n + 1)], bank(bk)),
                    wait=[f"{tg}_m{nt}"], inc="act", tag=f"{tg}_e{nt}")
                nt += 1
            yield
        p.wait("pe", [f"{tg}_e{nt-1}"])

    cb_gen = mul_jit(w2v, w3v, m1v, ["ld_w2", "ld_w3", "ld_eyb"], "cB")
    cd_gen = mul_jit(w1v, m1v, w3v, ["ld_w1", "cB_e15"], "cD")
    CBSET = {12, 14, 16, 18, 20, 22, 24, 26}
    CDSET = {28, 30}

    for t in range(NT):
        for j in range(NCH):
            p.emit("dve", lambda e, t=t, j=j: e.bn_stats(
                rn2[:, 6 * j:6 * j + 6], xfm3[:, j, 128 * t:128 * (t + 1)]),
                wait=([f"xld{t}"] if j == 0 else []))
        p.emit("dve", lambda e: e.drain())
        me = rn2[:, 1:48:6]
        mo = rn2[:, 4:48:6]
        Me = rn2[:, 2:48:6]
        Mo = rn2[:, 5:48:6]
        h96 = hstat[:, 96:104]
        h104 = hstat[:, 104:112]
        h112 = hstat[:, 112:120]
        h120 = hstat[:, 120:128]
        p.emit("dve", lambda e, me=me: e.tensor_tensor(h104, me, me, op=ALU.mult))
        p.emit("dve", lambda e, mo=mo: e.tensor_tensor(h112, mo, mo, op=ALU.mult))
        p.emit("dve", lambda e, me=me, mo=mo: e.tensor_tensor(h96, me, mo,
                                                              op=ALU.add))
        p.emit("dve", lambda e: e.tensor_tensor(h104, h104, h112, op=ALU.add))
        p.emit("dve", lambda e, Me=Me, Mo=Mo: e.tensor_tensor(h120, Me, Mo,
                                                              op=ALU.add))
        bases = ((96, 3 * t), (104, 3 * t + 1), (120, 3 * t + 2))
        for b_, _ in bases:
            hb = hstat[:, b_:b_ + 8]
            p.emit("dve", lambda e, hb=hb: e.tensor_tensor(
                hb[:, 0:4], hb[:, 0:4], hb[:, 4:8], op=ALU.add))
        for b_, _ in bases:
            hb = hstat[:, b_:b_ + 8]
            p.emit("dve", lambda e, hb=hb: e.tensor_tensor(
                hb[:, 0:2], hb[:, 0:2], hb[:, 2:4], op=ALU.add))
        for b_, dstc in bases:
            hb = hstat[:, b_:b_ + 8]
            p.emit("dve", lambda e, hb=hb, dstc=dstc: e.tensor_tensor(
                hstat[:, dstc:dstc + 1], hb[:, 0:1], hb[:, 1:2], op=ALU.add),
                inc=("dve" if b_ == 120 else None),
                tag=(f"xbn{t}" if b_ == 120 else None))
        for j in range(NCH):
            w = [f"xld{t}", "ld_eye", f"xbn{t}"]
            if j == 0 and t >= 1:
                w.append(f"xE{t-1}")
            col = 512 * (j // 4) + 128 * (j % 4)
            p.emit("pe", lambda e, t=t, j=j, c=col: e.transpose(
                ps[:, c:c + 128], xfm3[:, j, 128 * t:128 * (t + 1)], eyef[:, :]),
                wait=w, inc=("pe" if j == NCH - 1 else None),
                tag=(f"xT{t}" if j == NCH - 1 else None))
        for g in range(2):
            dst = xfm3[:, 4 * g:4 * g + 4, 128 * t:128 * t + 128]
            p.emit("act", lambda e, d=dst, c=512 * g: e.copy(
                d, ps[:, c:c + 512]),
                wait=[f"xT{t}"], inc=("act" if g == 1 else None),
                tag=(f"xE{t}" if g == 1 else None))
        if t in CBSET:
            next(cb_gen)
        if t in CDSET:
            next(cd_gen)

    msu = hstat[:, 0:96:3]
    m2u = hstat[:, 1:96:3]
    M2u = hstat[:, 2:96:3]
    s_, d_ = rn2[:, 0:32], rn2[:, 32:64]
    ex2 = hstat[:, 96:128]
    p.emit("dve", lambda e: e.drain(), wait=[f"xbn{NT-1}"])
    p.emit("dve", lambda e, m2u=m2u: e.tensor_scalar(
        ex2, m2u, 1.0 / 16, None, op0=ALU.mult))
    p.emit("dve", lambda e, msu=msu: e.tensor_scalar(
        s_, msu, 1.0 / 16, None, op0=ALU.mult))
    p.emit("dve", lambda e, M2u=M2u: e.scalar_tensor_tensor(
        ex2, M2u, 1.0 / 1024, ex2, op0=ALU.mult, op1=ALU.add))
    p.emit("dve", lambda e: e.tensor_tensor(d_, s_, s_, op=ALU.mult))
    p.emit("dve", lambda e: e.tensor_scalar(
        dmy[:, 0:1], dmy[:, 1:2], 1.0, None, op0=ALU.mult))
    p.emit("dve", lambda e: e.tensor_tensor(d_, ex2, d_, op=ALU.subtract))
    p.emit("dve", lambda e: e.tensor_scalar(
        dmy[:, 0:1], dmy[:, 1:2], 1.0, None, op0=ALU.mult))
    p.emit("dve", lambda e: e.tensor_scalar(d_, d_, EPS, None, op0=ALU.add))
    p.emit("dve", lambda e: e.drain())
    p.emit("dve", lambda e: e.reciprocal(d_, d_))
    p.emit("dve", lambda e: e.drain())
    p.emit("dve", lambda e: e.tensor_scalar(
        dmy[:, 0:1], dmy[:, 1:2], 1.0, None, op0=ALU.mult), inc="dve",
        tag="ln1v")
    p.emit("act", lambda e: e.activation(rntok[:, 0:64:2], d_, AF.Sqrt),
           wait=["ln1v"], inc="act", tag="ln1r")
    p.emit("dve", lambda e: e.tensor_scalar(
        rntok[:, 1:64:2], s_, -1.0, None, op0=ALU.mult),
        wait=["ln1r"], inc="dve", tag="ln1done")
    if debug:
        p.emit("pool", lambda e: e.dma_start(dbg["stats"][:, 0:64], rntok[:, :]),
               wait=["ln1done"], inc="ds")
        p.emit("pool", lambda e: e.dma_start(dbg["hs"][:, :], hstat[:, :]),
               wait=[f"xbn{NT-1}"], inc="ds")
        p.emit("pool", lambda e: e.dma_start(dbg["ln1"][:, :], rn2[:, 0:64]),
               wait=["ln1v"], inc="ds")
        p.emit("pool", lambda e: e.dma_start(dbg["xf"][:, :], xfm[:, :]),
               wait=[f"xE{NT-1}", f"xbn{NT-1}"], inc="ds")

    for _ in cb_gen:
        pass
    for _ in cd_gen:
        pass

    for j in range(8):
        p.emit("pe", lambda e, j=j: e.transpose(
            ps[:, 3080 + j:3080 + j + 1], fbrow[0:1, 128 * j:128 * (j + 1)],
            eyef[0:1, 0:1]),
            wait=(["ld_fb", "ld_eye", "cD_e15"] if j == 0 else []),
            inc=("pe" if j == 7 else None), tag=("fbT" if j == 7 else None))
    p.emit("dve", lambda e: e.tensor_copy(fbf[:, :], ps[:, 3080:3088]),
           wait=["fbT"], inc="dve", tag="fbE")
    p.emit("act", lambda e: e.activation(csrow, csrow, AF.Silu),
           wait=["ld_c"], inc="act", tag="silu")
    for j in range(8):
        p.emit("pe", lambda e, j=j: e.matmul(
            ps[:, 3072 + j:3072 + j + 1], csrow[0:1, 128 * j:128 * (j + 1)],
            eyeb[0:1, 0:1], start=True, stop=True),
            wait=(["silu", "ld_eyb", "fbE"] if j == 0 else []),
            inc=("pe" if j == 7 else None), tag=("csT" if j == 7 else None))
    p.emit("dve", lambda e: e.tensor_copy(csfm[:, :], ps[:, 3072:3080]),
           wait=["csT"], inc="dve", tag="csE")

    for pm in range(3):
        for ec in range(8):
            k = 8 * pm + ec
            w = [f"modm{(k-2)//8}_{(k-2)%8}"] if k >= 2 else ["cD_m15"]
            p.emit("pool", lambda e, pm=pm, ec=ec, s=ec % 2: e.dma_start(
                wmbuf[:, 2048 * s:2048 * (s + 1)],
                wm_h[128 * ec:128 * (ec + 1), 2048 * pm:2048 * (pm + 1)]),
                wait=w, inc=f"dwm{k % 2}", tag=f"wmld{k}")
            for n in range(4):
                w = [f"wmld{k}", "csE"]
                if pm > 0 and ec == 0 and n == 0:
                    w.append(f"modrowE{2*pm-1}")
                if pm == 0 and ec == 0 and n == 0:
                    w.append("cD_e15")
                p.emit("pe", lambda e, s=ec % 2, n=n, ec=ec: e.matmul(
                    ps[0:1, 512 * n:512 * (n + 1)], csfm[:, ec:ec + 1],
                    wmbuf[:, 2048 * s + 512 * n:2048 * s + 512 * (n + 1)],
                    start=(ec == 0), stop=(ec == 7)),
                    wait=w, inc=("pe" if n == 3 else None),
                    tag=(f"modm{pm}_{ec}" if n == 3 else None))
        if pm == 0:
            p.emit("pool", lambda e: e.dma_start(
                stA3[:, :, 0:64], wq_h.ap().rearrange("(c p) d -> p c d", p=128)),
                wait=["cD_m15"], inc="dqk", tag="ld_wq")
            p.emit("pool", lambda e: e.dma_start(
                stA3[:, :, 64:128], wv_h.ap().rearrange("(c p) d -> p c d", p=128)),
                inc="dqk", tag="ld_wv")
            p.emit("pool", lambda e: e.dma_start(
                stB3[:, :, :], wk_h.ap().rearrange("(c p) d -> p c d", p=128)),
                inc="dqk", tag="ld_wk")
        for hh2 in range(2):
            sp2 = 2 * pm + hh2
            p.emit("dve", lambda e, hh2=hh2: e.tensor_copy(
                modrow, ps[0:1, 1024 * hh2:1024 * (hh2 + 1)]),
                wait=[f"modm{pm}_7"], inc="dve", tag=f"modrowE{sp2}")
            for blk in range(8):
                w = []
                if blk == 0:
                    w = [f"modrowE{sp2}"]
                    if sp2 > 0:
                        w.append(f"mexE{sp2-1}")
                p.emit("pe", lambda e, blk=blk: e.transpose(
                    ps[:, 3072 + blk:3072 + blk + 1],
                    modrow[0:1, 128 * blk:128 * (blk + 1)], eyef[0:1, 0:1]),
                    wait=w, inc=("pe" if blk == 7 else None),
                    tag=(f"mexT{sp2}" if blk == 7 else None))
            p.emit("dve", lambda e, sp2=sp2: e.tensor_copy(
                modf[:, 8 * sp2:8 * (sp2 + 1)], ps[:, 3072:3080]),
                wait=[f"mexT{sp2}"], inc="dve", tag=f"mexE{sp2}")

    def derive(oi, oo, tagid, wtag):
        sa_i, ca_i, ga_i = oi, oi + 8, oi + 16
        pa, sp_, gp = oo, oo + 8, oo + 16
        p.emit("dve", lambda e: e.tensor_scalar(
            mvec[:, pa:pa + 8], modf[:, ca_i:ca_i + 8], 1.0, None, op0=ALU.add),
            wait=[wtag])
        p.emit("dve", lambda e: e.tensor_single_scalar(
            gmask[:, 0:8], modf[:, ga_i:ga_i + 8], 0.0, op=ALU.is_ge))
        p.emit("dve", lambda e: e.tensor_single_scalar(
            colsb[:, 0:8], modf[:, ga_i:ga_i + 8], 0.25, op=ALU.max))
        p.emit("dve", lambda e: e.tensor_single_scalar(
            colsb[:, 8:16], modf[:, ga_i:ga_i + 8], -0.25, op=ALU.min))
        p.emit("dve", lambda e: e.reciprocal(colsb[:, 8:16] if False else
                                             gmask[:, 8:16] if False else
                                             rn2[:, 0:8], mvec[:, pa:pa + 8]))
        p.emit("dve", lambda e: e.select(
            mvec[:, gp:gp + 8], gmask[:, 0:8], colsb[:, 0:8], colsb[:, 8:16]))
        p.emit("dve", lambda e: e.drain())
        p.emit("dve", lambda e: e.tensor_tensor(
            mvec[:, sp_:sp_ + 8], modf[:, sa_i:sa_i + 8], rn2[:, 0:8],
            op=ALU.mult),
            inc="dve", tag=tagid)

    derive(0, 0, "mva", "mexE2")
    derive(24, 24, "mvb", "mexE5")

    for ec in range(8):
        p.emit("dve", lambda e, ec=ec: e.tensor_scalar(
            stA3[:, ec, :], stA3[:, ec, :], mvec[:, ec:ec + 1], None, op0=ALU.mult),
            wait=(["mva", "ld_wq", "ld_wv", "ld_wk"] if ec == 0 else []))
        p.emit("dve", lambda e, ec=ec: e.tensor_scalar(
            stB3[:, ec, :], stB3[:, ec, :], mvec[:, ec:ec + 1], None, op0=ALU.mult),
            inc=("dve" if ec == 7 else None), tag=("stready" if ec == 7 else None))
    for ec in range(8):
        p.emit("dve", lambda e, ec=ec: e.tensor_scalar(
            w3v[:, ec, :], w3v[:, ec, :], mvec[:, 24 + ec:24 + ec + 1], None,
            op0=ALU.mult),
            wait=(["mvb", "cD_e15"] if ec == 0 else []),
            inc=("dve" if ec == 7 else None), tag=("wcready" if ec == 7 else None))
    if debug:
        p.emit("pool", lambda e: e.dma_start(dbg["mvec"][:, 0:48], mvec[:, :]),
               wait=["mva", "mvb"], inc="ds")
        p.emit("pool", lambda e: e.dma_start(dbg["mvec"][:, 48:96], modf[:, :]),
               inc="ds")
        p.emit("pool", lambda e: e.dma_start(dbg["wc"][:, :], w3s[:, :]),
               wait=["wcready"], inc="ds")
    for ec in range(8):
        p.emit("dve", lambda e, ec=ec: e.memset(extq[:, 2 * ec:2 * ec + 1], 1.0),
               wait=(["mva", "mvb"] if ec == 0 else []))
        p.emit("dve", lambda e, ec=ec: e.memset(extf[:, 2 * ec:2 * ec + 1], 1.0))
        p.emit("dve", lambda e, ec=ec: e.tensor_copy(
            extq[:, 2 * ec + 1:2 * ec + 2], mvec[:, 8 + ec:8 + ec + 1]))
        p.emit("dve", lambda e, ec=ec: e.tensor_copy(
            extf[:, 2 * ec + 1:2 * ec + 2], mvec[:, 32 + ec:32 + ec + 1]),
            inc=("dve" if ec == 7 else None), tag=("extready" if ec == 7 else None))

    prevtag = None
    prevfix = None
    prevwA = None
    prevwB = None
    for q in range(4):
        for r in range(2):
            for u in range(8):
                tgl = 2 * (8 * q + u) + r
                w = []
                if u == 0:
                    w = ["ln1done"]
                    if q == 0 and r == 0:
                        w += ["mexE5", "fbE", "csE"]
                    if r == 1:
                        w.append(f"q_repE{q}_0")
                    elif q > 0:
                        w.append(f"q_repE{q-1}_1")
                p.emit("pe", lambda e, u=u, tgl=tgl: e.transpose(
                    ps[0:1, 2560 + 128 * u:2560 + 128 * (u + 1)],
                    rntok[:, tgl:tgl + 1], eyef[:, :]),
                    wait=w, inc=("pe" if u == 7 else None),
                    tag=(f"q_rT{q}_{r}" if u == 7 else None))
            p.emit("dve", lambda e: e.tensor_copy(rowbf[0:1, 0:1024],
                                                  ps[0:1, 2560:3584]),
                   wait=[f"q_rT{q}_{r}"], inc="dve", tag=f"q_rE{q}_{r}")
            for n in range(2):
                p.emit("pe", lambda e, n=n: e.matmul(
                    ps[:, 2560 + 512 * n:2560 + 512 * (n + 1)], oner[:, :],
                    rowbf[0:1, 512 * n:512 * (n + 1)], start=True, stop=True),
                    wait=([f"q_rE{q}_{r}", "consts"] if n == 0 else []),
                    inc=("pe" if n == 1 else None),
                    tag=(f"q_bT{q}_{r}" if n == 1 else None))
            p.emit("dve", lambda e, r=r: e.tensor_copy(
                reps[:, 1024 * r:1024 * (r + 1)], ps[:, 2560:3584]),
                wait=[f"q_bT{q}_{r}"], inc="dve", tag=f"q_repE{q}_{r}")

        for hh in range(2):
            cg = 1024 * q + 512 * hh
            for ec in range(8):
                w = []
                if ec == 0:
                    w = [f"xE{NT-1}", "stready"]
                    if prevtag is not None:
                        w.append(prevtag)
                p.emit("dve", lambda e, ec=ec, cg=cg: e.tensor_copy(
                    qb3[:, ec, :], xfm3[:, ec, cg:cg + 512]),
                    wait=w, inc="dve", tag=f"q_c{q}_{hh}_{ec}")
            for ec in range(8):
                w = [f"q_c{q}_{hh}_{ec}"]
                if ec == 0 and prevfix is not None:
                    w.append(prevfix)
                p.emit("pe", lambda e, ec=ec, hh=hh: e.matmul(
                    bank(hh), stA3[:, ec, :], qb3[:, ec, :],
                    start=(ec == 0), stop=(ec == 7)),
                    wait=w, inc=("pe" if ec == 7 else None),
                    tag=(f"q_A{q}_{hh}" if ec == 7 else None))
            for ec in range(8):
                p.emit("pe", lambda e, ec=ec, hh=hh: e.matmul(
                    bank(3 + hh, 0, 64), stB3[:, ec, :], qb3[:, ec, :],
                    start=(ec == 0), stop=(ec == 7)),
                    inc=("pe" if ec == 7 else None),
                    tag=(f"q_B{q}_{hh}" if ec == 7 else None))
            if q == 0 and hh == 0:
                for ec in range(8):
                    p.emit("pe", lambda e, ec=ec: e.matmul(
                        ps[:, 1024:1026], stA3[:, ec, :], extq[:, 2 * ec:2 * ec + 2],
                        start=(ec == 0), stop=(ec == 7)),
                        wait=(["extready"] if ec == 0 else []),
                        inc=("pe" if ec == 7 else None),
                        tag=("q_exA" if ec == 7 else None))
                for ec in range(8):
                    p.emit("pe", lambda e, ec=ec: e.matmul(
                        ps[0:64, 1026:1028], stB3[:, ec, :],
                        extq[:, 2 * ec:2 * ec + 2],
                        start=(ec == 0), stop=(ec == 7)),
                        inc=("pe" if ec == 7 else None),
                        tag=("q_exB" if ec == 7 else None))
                p.emit("dve", lambda e: e.tensor_copy(uqv[:, :], ps[:, 1024:1026]),
                       wait=["q_exA"])
                p.emit("dve", lambda e: e.tensor_copy(ukk[:, :], ps[0:64, 1026:1028]),
                       wait=["q_exB"], inc="dve", tag="q_ext")

            cs0 = 512 * hh
            b7 = ps[:, 3584:4096]
            b2f = ps[:, 1024:1536]
            wA = [f"q_A{q}_{hh}", f"q_repE{q}_1", "q_ext"]
            p.emit("dve", lambda e, hh=hh, cs0=cs0, b7=b7: e.scalar_tensor_tensor(
                b7, reps[:, 1024 + cs0:1024 + cs0 + 512], uqv[:, 0:1],
                bank(hh), op0=ALU.mult, op1=ALU.add),
                wait=wA)
            wA2 = [prevwB] if prevwB is not None else []
            p.emit("dve", lambda e, cs0=cs0, b7=b7, b2f=b2f: e.tensor_tensor(
                b2f, b7, reps[:, cs0:cs0 + 512], op=ALU.mult),
                wait=wA2, inc="dve", tag=f"q_fA{q}_{hh}")
            p.emit("dve", lambda e, cg=cg, b2f=b2f: e.tensor_scalar(
                qv[:, cg:cg + 512], b2f, uqv[:, 1:2], None, op0=ALU.add),
                wait=[f"q_fA{q}_{hh}"], inc="dve", tag=f"q_wA{q}_{hh}")
            p.emit("dve", lambda e, hh=hh, cs0=cs0, b7=b7: e.scalar_tensor_tensor(
                b7[0:64, :], reps[0:64, 1024 + cs0:1024 + cs0 + 512], ukk[:, 0:1],
                bank(3 + hh, 0, 64), op0=ALU.mult, op1=ALU.add),
                wait=[f"q_B{q}_{hh}"], inc="dve", tag=f"q_sB{q}_{hh}")
            p.emit("dve", lambda e, cs0=cs0, b7=b7, b2f=b2f: e.tensor_tensor(
                b2f[0:64, :], b7[0:64, :], reps[0:64, cs0:cs0 + 512], op=ALU.mult),
                wait=[f"q_wA{q}_{hh}"], inc="dve", tag=f"q_fB{q}_{hh}")
            p.emit("dve", lambda e, cg=cg, b2f=b2f: e.tensor_scalar(
                kk[0:64, cg:cg + 512], b2f[0:64, :], ukk[:, 1:2], None, op0=ALU.add),
                wait=[f"q_fB{q}_{hh}"], inc="dve", tag=f"q_wB{q}_{hh}")
            prevtag = f"q_B{q}_{hh}"
            prevfix = f"q_sB{q}_{hh}"
            prevwA = f"q_wA{q}_{hh}"
            prevwB = f"q_wB{q}_{hh}"

    if debug:
        p.emit("pool", lambda e: e.dma_start(dbg["qv"][:, :], qv[:, :]),
               wait=["q_wA3_1", "q_wB3_1"], inc="ds")
        p.emit("pool", lambda e: e.dma_start(dbg["kk"][:, :], kk[:, :]), inc="ds")

    for g in range(8):
        for u in range(4):
            c = 4 * g + u
            w = []
            if g == 0 and u == 0:
                w = ["q_wA3_1", "q_wB3_1", "ld_eyb", "q_repE3_1"]
            if u == 0 and g >= 2:
                w.append(f"vE{g-2}")
            bkc = 2560 + 512 * (g % 2)
            p.emit("pe", lambda e, c=c, u=u, bkc=bkc: e.matmul(
                ps[:, bkc + 64 * u:bkc + 64 * (u + 1)],
                qv[64:128, 128 * c:128 * (c + 1)], eyeb[64:128, 64:128],
                start=True, stop=True),
                wait=w, inc=("pe" if u == 3 else None),
                tag=(f"vT{g}" if u == 3 else None))
        p.emit("dve", lambda e, g=g, bkc=2560 + 512 * (g % 2): e.tensor_copy(
            vtok[:, 256 * g:256 * (g + 1)], ps[:, bkc:bkc + 256]),
            wait=[f"vT{g}"], inc="dve", tag=f"vE{g}")

    rowq = w2s[0:1, 0:T]
    qm2 = w2s[0:1, T:T + 1]
    kcols = rn2[:, 0:32]
    qm2c = rn2[:, 32:33]
    negall = rntok[:, 0:32]
    p.emit("dve", lambda e: e.tensor_tensor(
        a3[0:64, 1, :], kk[0:64, :], kk[0:64, :], op=ALU.mult),
        wait=["q_wB3_1", "cD_m15", "modm2_7"])
    p.emit("dve", lambda e: e.tensor_tensor(
        a3[0:64, 0, :], qv[0:64, :], qv[0:64, :], op=ALU.mult),
        wait=["q_wA3_1"], inc="dve", tag="nsq")
    for n in range(8):
        p.emit("pe", lambda e, n=n: e.matmul(
            ps[0:1, 512 * n:512 * (n + 1)], onec[0:64, 0:1],
            a3[0:64, 1, 512 * n:512 * (n + 1)], start=True, stop=True),
            wait=(["nsq", "vE7", "q_fB3_1", "q_ext"] if n == 0 else []),
            inc=("pe" if n == 7 else None), tag=("knT" if n == 7 else None))
    p.emit("dve", lambda e: e.tensor_copy(rowq, ps[0:1, 0:T]),
           wait=["knT"], inc="dve", tag="knE")
    for blk in range(32):
        w = ["knE"] if blk == 0 else []
        col = 2048 + 8 * (blk // 8) + (blk % 8)
        p.emit("pe", lambda e, blk=blk, col=col: e.matmul(
            ps[:, col:col + 1],
            rowq[0:1, 128 * blk:128 * (blk + 1)], eyeb[0:1, 0:1],
            start=True, stop=True),
            wait=w, inc=("pe" if blk == 31 else None),
            tag=("kcT" if blk == 31 else None))
    p.emit("dve", lambda e: e.tensor_copy(kcols, ps[:, 2048:2080]),
        wait=["kcT"], inc="dve", tag="kcE")
    for n in range(8):
        p.emit("pe", lambda e, n=n: e.matmul(
            ps[0:1, 512 * n:512 * (n + 1)], onec[0:64, 0:1],
            a3[0:64, 0, 512 * n:512 * (n + 1)], start=True, stop=True),
            wait=(["knE"] if n == 0 else []),
            inc=("pe" if n == 7 else None), tag=("qnT" if n == 7 else None))
    p.emit("dve", lambda e: e.tensor_copy(rowq, ps[0:1, 0:T]),
           wait=["qnT", "kcE"], inc="dve", tag="qnE")
    p.emit("dve", lambda e: e.reduce_max(qm2, rowq, axis=AX.X))
    p.emit("dve", lambda e: e.drain())
    p.emit("dve", lambda e: e.tensor_scalar(
        dmy[:, 0:1], dmy[:, 1:2], 1.0, None, op0=ALU.mult), inc="dve",
        tag="qmx")
    p.emit("pe", lambda e: e.matmul(ps[:, 2080:2081], oner[:, :], qm2,
                                    start=True, stop=True),
           wait=["qmx"], inc="pe", tag="qbT")
    p.emit("dve", lambda e: e.tensor_copy(qm2c, ps[:, 2080:2081]),
           wait=["qbT"], inc="dve", tag="qbE")
    p.emit("dve", lambda e: e.tensor_scalar(
        negall, kcols, qm2c, None, op0=ALU.mult))
    p.emit("dve", lambda e: e.tensor_scalar(
        dmy[:, 0:1], dmy[:, 1:2], 1.0, None, op0=ALU.mult))
    p.emit("dve", lambda e: e.tensor_scalar(
        dmy[:, 0:1], dmy[:, 1:2], 1.0, None, op0=ALU.mult), inc="dve",
        tag="mupre")
    p.emit("act", lambda e: e.activation(rn2[:, 0:32], negall, AF.Sqrt),
           wait=["mupre"], inc="act", tag="musq")
    p.emit("dve", lambda e: e.tensor_scalar(
        negall, rn2[:, 0:32], -1.01, None, op0=ALU.mult),
        wait=["musq"], inc="dve", tag="muall")
    p.emit("dve", lambda e: e.memset(kk[64:128, :], 0.0),
           wait=["cD_m15"], inc="dve", tag="kkz")

    p.wait("pe", ["vE7", "q_fB3_1", "q_ext", "muall", "qbE", "kkz"])
    def emit_av(c):
        cs = c % 2
        for nn in range(4):
            for cg2 in range(2):
                w = [f"vs{c}"]
                p.emit("pe", lambda e, c=c, cs=cs, nn=nn, cg2=cg2: e.matmul(
                    ps[64 * cg2:64 * (cg2 + 1), 512 * nn:512 * (nn + 1)],
                    vscb[:, 64 * cs:64 * (cs + 1)],
                    a3[:, cs, 2048 * cg2 + 512 * nn:2048 * cg2 + 512 * (nn + 1)],
                    start=(c == 0), stop=(c == NKC - 1),
                    tile_position=(0, 64 * cg2)),
                    wait=w, inc=("pe" if (nn == 3 and cg2 == 1) else None),
                    tag=(f"AV{c}" if (nn == 3 and cg2 == 1) else None))

    for c in range(NKC):
        cs = c % 2
        for s in range(4):
            gc = 2048 + 1024 * (s % 2)
            if s == 2 and c >= 1:
                emit_av(c - 1)
            for nn in range(2):
                i0 = 1024 * s + 512 * nn
                if c == 0 and s < 2:
                    w = []
                elif s >= 2:
                    w = [f"exp{c}_{s-2}"]
                else:
                    w = [f"exp{c-1}_{s+2}"]
                p.emit("pe", lambda e, c=c, i0=i0, gc=gc, nn=nn: e.matmul(
                    ps[:, gc + 512 * nn:gc + 512 * (nn + 1)],
                    kk[:, 128 * c:128 * (c + 1)], qv[:, i0:i0 + 512],
                    start=True, stop=True),
                    wait=w, inc=("pe" if nn == 1 else None),
                    tag=(f"S{c}_{s}" if nn == 1 else None))
            w = [f"S{c}_{s}", "muall"]
            if c >= 2:
                w += [f"AV{c-2}", f"vs{c-2}"]
            p.emit("act", lambda e, c=c, s=s, cs=cs, gc=gc: e.activation(
                a3[:, cs, 1024 * s:1024 * (s + 1)], ps[:, gc:gc + 1024], AF.Exp,
                bias=negall[:, c:c + 1],
                accum_out=zpart[:, 4 * cs + s:4 * cs + s + 1]),
                wait=w, inc="act", tag=f"exp{c}_{s}")
        p.emit("dve", lambda e, cs=cs: e.tensor_tensor(
            zval[:, cs:cs + 1], zpart[:, 4 * cs:4 * cs + 1],
            zpart[:, 4 * cs + 1:4 * cs + 2], op=ALU.add),
            wait=[f"exp{c}_{s2}" for s2 in range(4)])
        p.emit("dve", lambda e, cs=cs: e.tensor_tensor(
            zrec[:, cs:cs + 1], zpart[:, 4 * cs + 2:4 * cs + 3],
            zpart[:, 4 * cs + 3:4 * cs + 4], op=ALU.add))
        p.emit("dve", lambda e: e.tensor_scalar(
            dmy[:, 0:1], dmy[:, 1:2], 1.0, None, op0=ALU.mult))
        p.emit("dve", lambda e, cs=cs: e.tensor_tensor(
            zval[:, cs:cs + 1], zval[:, cs:cs + 1], zrec[:, cs:cs + 1],
            op=ALU.add))
        p.emit("dve", lambda e: e.drain())
        p.emit("dve", lambda e, cs=cs: e.reciprocal(zrec[:, cs:cs + 1],
                                                    zval[:, cs:cs + 1]))
        p.emit("dve", lambda e: e.drain())
        w = [f"vE{c//4}"]
        if c >= 2:
            w.append(f"AV{c-2}")
        p.emit("dve", lambda e, c=c, cs=cs: e.tensor_scalar(
            vscb[:, 64 * cs:64 * (cs + 1)], vtok[:, 64 * c:64 * (c + 1)],
            zrec[:, cs:cs + 1], None, op0=ALU.mult),
            wait=w, inc="dve", tag=f"vs{c}")
        if debug:
            p.emit("pool", lambda e, c=c, cs=cs: e.dma_start(
                dbg["z"][:, 2 * c:2 * c + 1], zval[:, cs:cs + 1],
                allow_slow_non_contiguous=True),
                wait=[f"vs{c}"], inc="ds")
            p.emit("pool", lambda e, c=c: e.dma_start(
                dbg["z"][:, 2 * c + 1:2 * c + 2], negall[:, c:c + 1],
                allow_slow_non_contiguous=True), inc="ds")
    emit_av(NKC - 1)

    p.emit("dve", lambda e: e.tensor_copy(avs[:, :], ps[:, 0:2048]),
           wait=[f"AV{NKC-1}"], inc="dve", tag="avE")
    for i, (dp, dc0, sp0) in enumerate([(0, 0, 0), (0, 2048, 64),
                                        (64, 0, 0), (64, 2048, 64)]):
        p.emit("pool", lambda e, dp=dp, dc0=dc0, sp0=sp0: e.dma_start(
            avd[dp:dp + 64, dc0:dc0 + 2048], avs[sp0:sp0 + 64, 0:2048]),
            wait=(["avE"] if i == 0 else []), inc="ds",
            tag=("avdup" if i == 3 else None))
    if debug:
        p.emit("pool", lambda e: e.dma_start(dbg["av"][:, :], avd[:, :]),
               wait=["avdup"], inc="ds")

    for q in range(4):
        for ec in range(8):
            slot = ec % 2
            if q == 0:
                p.emit("dve", lambda e, j=ec: e.scalar_tensor_tensor(
                    xfm3[:, j, :], avd[:, :], mvec[:, 16 + j:16 + j + 1],
                    xfm3[:, j, :], op0=ALU.mult, op1=ALU.add),
                    wait=(["avdup", "mva"] if ec == 0 else []),
                    inc="dve", tag=f"res1_{ec}")
                if debug:
                    p.emit("pool", lambda e, j=ec: e.dma_start(
                        dbg["x1"][:, T * j:T * (j + 1)], xfm3[:, j, :]),
                        wait=[f"res1_{ec}"], inc="ds")
            w = [f"res1_{ec}"]
            if q > 0 or ec >= 2:
                w.append(f"lnm{q}_{ec-2}" if ec >= 2 else f"lnm{q-1}_{6+ec}")
            p.emit("dve", lambda e, ec=ec, q=q, s=slot: e.tensor_copy(
                plbuf[:, 1024 * s:1024 * (s + 1)],
                xfm3[:, ec, 1024 * q:1024 * (q + 1)]),
                wait=w)
            p.emit("dve", lambda e, s=slot: e.tensor_tensor(
                sqb[:, 1024 * s:1024 * (s + 1)], plbuf[:, 1024 * s:1024 * (s + 1)],
                plbuf[:, 1024 * s:1024 * (s + 1)], op=ALU.mult),
                inc="dve", tag=f"lnc{q}_{ec}")
            for n in range(2):
                w = [f"lnc{q}_{ec}", "consts"]
                if q == 0 and ec == 0 and n == 0:
                    w.append("avE")
                if q > 0 and ec == 0 and n == 0:
                    w.append(f"lnrowE{q-1}")
                p.emit("pe", lambda e, s=slot, n=n, ec=ec: e.matmul(
                    ps[0:1, 512 * n:512 * (n + 1)], onec[:, 0:1],
                    plbuf[:, 1024 * s + 512 * n:1024 * s + 512 * (n + 1)],
                    start=(ec == 0), stop=(ec == 7)),
                    wait=w)
                p.emit("pe", lambda e, s=slot, n=n, ec=ec: e.matmul(
                    ps[0:1, 1024 + 512 * n:1024 + 512 * (n + 1)], onec[:, 0:1],
                    sqb[:, 1024 * s + 512 * n:1024 * s + 512 * (n + 1)],
                    start=(ec == 0), stop=(ec == 7)),
                    inc=("pe" if n == 1 else None),
                    tag=(f"lnm{q}_{ec}" if n == 1 else None))
        p.emit("dve", lambda e: e.tensor_copy(ln2rows, ps[0:1, 0:2048]),
               wait=[f"lnm{q}_7"], inc="dve", tag=f"lnrowE{q}")
        for u in range(16):
            w = []
            if u == 0:
                w = [f"lnrowE{q}"]
                if q > 0:
                    w.append(f"lncolE{q-1}")
            p.emit("pe", lambda e, u=u: e.matmul(
                ps[:, 2048 + u:2048 + u + 1], ln2rows[0:1, 128 * u:128 * (u + 1)],
                eyeb[0:1, 0:1], start=True, stop=True),
                wait=w, inc=("pe" if u == 15 else None),
                tag=(f"lncT{q}" if u == 15 else None))
        p.emit("dve", lambda e: e.tensor_copy(colsb[:, :], ps[:, 2048:2064]),
               wait=[f"lncT{q}"], inc="dve", tag=f"lncolE{q}")
        mq = colsb[:, 0:8]
        sq_ = colsb[:, 8:16]
        p.emit("dve", lambda e, mq=mq: e.tensor_scalar(mq, mq, 1.0 / E, None,
                                                       op0=ALU.mult))
        p.emit("dve", lambda e, sq_=sq_: e.tensor_scalar(sq_, sq_, 1.0 / E, None,
                                                         op0=ALU.mult))
        p.emit("dve", lambda e, q=q, mq=mq: e.tensor_copy(
            rn2[:, 16 * q:16 * q + 16:2], mq))
        p.emit("dve", lambda e, mq=mq: e.tensor_tensor(mq, mq, mq, op=ALU.mult))
        p.emit("dve", lambda e: e.tensor_scalar(
            dmy[:, 0:1], dmy[:, 1:2], 1.0, None, op0=ALU.mult))
        p.emit("dve", lambda e, mq=mq, sq_=sq_: e.tensor_tensor(
            sq_, sq_, mq, op=ALU.subtract))
        p.emit("dve", lambda e: e.tensor_scalar(
            dmy[:, 0:1], dmy[:, 1:2], 1.0, None, op0=ALU.mult))
        p.emit("dve", lambda e, sq_=sq_: e.tensor_scalar(sq_, sq_, EPS, None,
                                                         op0=ALU.add))
        p.emit("dve", lambda e: e.drain())
        p.emit("dve", lambda e, q=q, sq_=sq_: e.reciprocal(
            rn2[:, 16 * q + 1:16 * q + 16:2], sq_))
        p.emit("dve", lambda e: e.drain())
        p.emit("dve", lambda e: e.tensor_scalar(
            dmy[:, 0:1], dmy[:, 1:2], 1.0, None, op0=ALU.mult),
            inc="dve", tag=f"lnv{q}")
    p.emit("act", lambda e: e.activation(rn2[:, 1:64:2], rn2[:, 1:64:2], AF.Sqrt),
           wait=[f"lnv{qq}" for qq in range(4)], inc="act", tag="ln2r")
    p.emit("dve", lambda e: e.tensor_tensor(
        hstat[:, 64:96], rn2[:, 0:64:2], rn2[:, 1:64:2], op=ALU.mult),
        wait=["ln2r"])
    p.emit("dve", lambda e: e.tensor_scalar(
        dmy[:, 0:1], dmy[:, 1:2], 1.0, None, op0=ALU.mult))
    p.emit("dve", lambda e: e.tensor_scalar(
        dmy[:, 0:1], dmy[:, 1:2], 1.0, None, op0=ALU.mult))
    p.emit("dve", lambda e: e.tensor_scalar(
        rn2[:, 0:64:2], hstat[:, 64:96], -1.0, None, op0=ALU.mult),
        inc="dve", tag="ln2nm")

    for dc in range(8):
        for ec in range(8):
            w = []
            if dc == 0 and ec == 0:
                w = ["wcready", "extready", "lnm3_7", "lnrowE3"]
            p.emit("pe", lambda e, dc=dc, ec=ec: e.matmul(
                ps[:, 2 * dc:2 * dc + 2], w3v[:, ec, 128 * dc:128 * (dc + 1)],
                extf[:, 2 * ec:2 * ec + 2], start=(ec == 0), stop=(ec == 7)),
                wait=w, inc=("pe" if (ec == 7 and dc == 7) else None),
                tag=("fext" if (ec == 7 and dc == 7) else None))
    p.emit("dve", lambda e: e.tensor_copy(u3b[:, :], ps[:, 0:16]),
           wait=["fext"], inc="dve", tag="fextE")
    p.emit("dve", lambda e: e.tensor_scalar(
        dmy[:, 0:1], dmy[:, 1:2], 1.0, None, op0=ALU.mult))
    p.emit("dve", lambda e: e.tensor_scalar(
        dmy[:, 0:1], dmy[:, 1:2], 1.0, None, op0=ALU.mult))
    p.emit("dve", lambda e: e.tensor_tensor(bias3[:, :], u3b[:, 1:16:2], fbf[:, :],
                                            op=ALU.add),
           wait=["fextE", "fbE"], inc="dve", tag="bias3r")
    if debug:
        p.emit("pool", lambda e: e.dma_start(dbg["u3"][:, 0:16], u3b[:, :]),
               wait=["fextE"], inc="ds")
        p.emit("pool", lambda e: e.dma_start(dbg["u3"][:, 16:24], bias3[:, :]),
               wait=["bias3r"], inc="ds")

    for q in range(4):
        for r in range(2):
            roff = 2048 + 1024 * r
            for u in range(8):
                tq = 16 * q + 2 * u + (1 - r)
                w = []
                if u == 0:
                    w = ["ln2nm"]
                    if r == 0:
                        w.append("fextE" if q == 0 else f"tanh{q-1}_7")
                    if r == 0 and q > 0:
                        w += [f"oE{8*q-2}", f"oEa{8*q-2}"]
                p.emit("pe", lambda e, u=u, tq=tq, roff=roff: e.transpose(
                    ps[0:1, roff + 128 * u:roff + 128 * (u + 1)],
                    rn2[:, tq:tq + 1], eyef[:, :]),
                    wait=w, inc=("pe" if u == 7 else None),
                    tag=(f"f_rT{q}_{r}" if u == 7 else None))
            p.emit("dve", lambda e, r=r, roff=roff: e.tensor_copy(
                rowbf[0:1, 1024 * r:1024 * (r + 1)], ps[0:1, roff:roff + 1024]),
                wait=[f"f_rT{q}_{r}"], inc="dve", tag=f"f_rE{q}_{r}")
        for r in range(2):
            roff = 2048 + 1024 * r
            for n in range(2):
                p.emit("pe", lambda e, r=r, n=n, roff=roff: e.matmul(
                    ps[:, roff + 512 * n:roff + 512 * (n + 1)], oner[:, :],
                    rowbf[0:1, 1024 * r + 512 * n:1024 * r + 512 * (n + 1)],
                    start=True, stop=True),
                    wait=([f"f_rE{q}_1"] if (r == 0 and n == 0) else []),
                    inc=("pe" if n == 1 else None),
                    tag=(f"f_bT{q}_{r}" if n == 1 else None))
            p.emit("dve", lambda e, r=r, roff=roff: e.tensor_copy(
                reps[:, 1024 * r:1024 * (r + 1)], ps[:, roff:roff + 1024]),
                wait=[f"f_bT{q}_{r}"], inc="dve", tag=f"f_repE{q}_{r}")
        for ec in range(8):
            wsc = []
            if ec == 0:
                wsc = [f"f_repE{q}_1"]
                if q > 0:
                    wsc.append(f"fP{q-1}_7")
            p.emit("dve", lambda e, ec=ec, q=q: e.tensor_tensor(
                xs3[:, ec, :], xfm3[:, ec, 1024 * q:1024 * (q + 1)],
                reps[:, 0:1024], op=ALU.mult),
                wait=wsc, inc="dve", tag=f"f_sc{q}_{ec}")
        for dc in range(8):
            pb = dc % 2
            pc0 = 1024 * pb
            t20 = 2048 + 1024 * pb
            for ec in range(8):
                for n in range(2):
                    w = []
                    if dc == 0 and n == 0:
                        w.append(f"f_sc{q}_{ec}")
                    if ec == 0 and n == 0:
                        if dc >= 1:
                            w.append(f"f_sc{q}_7")
                        if dc >= 2:
                            w.append(f"tanh{q}_{dc-2}")
                        elif q > 0:
                            w.append(f"tanh{q-1}_{6+dc}")
                            w.append(f"oE{8*q-3}")
                            w.append(f"oEa{8*q-3}")
                        if q == 0 and dc <= 1:
                            w.append("fextE")
                        if dc <= 1:
                            w.append(f"f_repE{q}_1")
                    p.emit("pe", lambda e, dc=dc, ec=ec, n=n, pc0=pc0: e.matmul(
                        ps[:, pc0 + 512 * n:pc0 + 512 * (n + 1)],
                        w3v[:, ec, 128 * dc:128 * (dc + 1)],
                        xs3[:, ec, 512 * n:512 * (n + 1)],
                        start=(ec == 0), stop=(ec == 7)),
                        wait=w, inc=("pe" if (ec == 7 and n == 1) else None),
                        tag=(f"fP{q}_{dc}" if (ec == 7 and n == 1) else None))
            p.emit("dve", lambda e, pc0=pc0, t20=t20, dc=dc: e.scalar_tensor_tensor(
                ps[:, t20:t20 + 1024], reps[:, 1024:2048], u3b[:, 2 * dc:2 * dc + 1],
                ps[:, pc0:pc0 + 1024], op0=ALU.mult, op1=ALU.add),
                wait=[f"fP{q}_{dc}", f"f_repE{q}_1", "bias3r"],
                inc="dve", tag=f"fT2{q}_{dc}")
            wth = [f"fT2{q}_{dc}"]
            if dc >= 2:
                wth.append(f"res2_{q}_{dc-2}")
            elif q > 0:
                wth.append(f"res2_{q-1}_{6+dc}")
            elif dc == 0:
                wth.append("avdup")
                wth.append(f"res1_{NCH-1}")
            p.emit("act", lambda e, dc=dc, t20=t20: e.activation(
                tnh[:, 1024 * (dc % 2):1024 * (dc % 2) + 1024], ps[:, t20:t20 + 1024],
                AF.Tanh, bias=bias3[:, dc:dc + 1]),
                wait=wth, inc="act", tag=f"tanh{q}_{dc}")
            p.emit("dve", lambda e, dc=dc, q=q: e.scalar_tensor_tensor(
                xfm3[:, dc, 1024 * q:1024 * (q + 1)],
                tnh[:, 1024 * (dc % 2):1024 * (dc % 2) + 1024],
                mvec[:, 40 + dc:40 + dc + 1], xfm3[:, dc, 1024 * q:1024 * (q + 1)],
                op0=ALU.mult, op1=ALU.add),
                wait=[f"tanh{q}_{dc}", "mvb"], inc="dve", tag=f"res2_{q}_{dc}")

        for ti in range(8):
            t = 8 * q + ti
            b0c = 1024 * (ti % 4)
            for j in range(NCH):
                w = []
                if j == 0:
                    if ti == 0:
                        w = [f"fT2{q}_7", f"tanh{q}_7"] + \
                            [f"res2_{q}_{k}" for k in range(NCH)]
                    if ti >= 4:
                        w = [f"oE{t-4}", f"oEa{t-4}"]
                col = b0c + 512 * (j // 4) + 128 * (j % 4)
                p.emit("pe", lambda e, t=t, j=j, col=col: e.transpose(
                    ps[:, col:col + 128], xfm3[:, j, 128 * t:128 * (t + 1)],
                    eyef[:, :]),
                    wait=w, inc=("pe" if j == NCH - 1 else None),
                    tag=(f"oT{t}" if j == NCH - 1 else None))
            p.emit("dve", lambda e, t=t, b0c=b0c: e.tensor_copy(
                xfm3[:, 0:4, 128 * t:128 * t + 128], ps[:, b0c:b0c + 512]),
                wait=[f"oT{t}"], inc="dve", tag=f"oE{t}")
            p.emit("act", lambda e, t=t, b0c=b0c: e.copy(
                xfm3[:, 4:8, 128 * t:128 * t + 128],
                ps[:, b0c + 512:b0c + 1024]),
                wait=[f"oT{t}"], inc="act", tag=f"oEa{t}")
            p.emit("sp", lambda e, t=t: e.dma_start(
                out_h[128 * t:128 * (t + 1), :].rearrange("p (c m) -> p c m",
                                                          m=128),
                xfm3[:, :, 128 * t:128 * (t + 1)]),
                wait=[f"oE{t}", f"oEa{t}"], inc="do", tag=f"ost{t}")

    p.wait("sp", [f"ost{t}" for t in range(NT)])

    p.lower(nc, semh)
    es.close()
    return nc


_NC_CACHE = {}


def _get_nc(debug=False):
    if debug not in _NC_CACHE:
        _NC_CACHE[debug] = build(debug)
    return _NC_CACHE[debug]


def make_in_maps(x, wq, wk, wv, w1, w2, w3, fb, c, wm):
    eye = np.eye(128, dtype=np.float32)
    eyb = np.eye(128).astype(ml_dtypes.bfloat16)
    in_maps = []
    for b in range(B):
        in_maps.append({
            "x": x[b], "wq": wq, "wk": wk, "wv": wv,
            "w1": w1, "w2": w2, "w3": w3, "fbias": fb,
            "crow": c[b:b + 1], "wmod": wm, "eye": eye, "eyeb": eyb,
        })
    return in_maps


def kernel(L0_attn_x, L0_attn_Wq, L0_attn_Wk, L0_attn_Wv, L0_ffn_weight,
           L0_ffn_aux_weight1, L0_ffn_aux_weight2, L0_ffn_bias, L0_c, L0_wmod):
    f = lambda a: np.ascontiguousarray(np.asarray(a, dtype=np.float32))
    x = f(L0_attn_x)
    in_maps = make_in_maps(x, f(L0_attn_Wq), f(L0_attn_Wk), f(L0_attn_Wv),
                           f(L0_ffn_weight), f(L0_ffn_aux_weight1),
                           f(L0_ffn_aux_weight2), f(L0_ffn_bias).reshape(1, E),
                           f(L0_c), f(L0_wmod))
    nc = _get_nc(False)
    res = run_bass_kernel_spmd(nc, in_maps, core_ids=list(range(B)))
    return np.stack([res.results[b]["out"] for b in range(B)], axis=0)
